# version 02
# baseline (speedup 1.0000x reference)
import contextlib
import math
import numpy as np
import ml_dtypes
import concourse.bass as bass
import concourse.mybir as mybir
from concourse.ap import AP as APClass
from concourse.bass_utils import run_bass_kernel_spmd

F32 = mybir.dt.float32
BF16 = mybir.dt.bfloat16
I32 = mybir.dt.int32
I16 = mybir.dt.int16
AF = mybir.ActivationFunctionType
ALU = mybir.AluOpType
AX = mybir.AxisListType
BF = ml_dtypes.bfloat16

NCORES = 8
ENGINES = ("sync", "scalar", "vector", "tensor", "gpsimd")

import hashlib
import os
import shutil
import concourse.bass_utils as _bu

if not getattr(_bu, "_neff_cache_installed", False):
    _orig_compile = _bu.compile_bir_kernel

    def _cached_compile(bir_json, tmpdir, neff_name="file.neff"):
        cdir = os.environ.get("BASS_NEFF_CACHE", "/tmp/bass_neff_cache")
        try:
            os.makedirs(cdir, exist_ok=True)
            key = hashlib.sha256(bir_json).hexdigest()
            cpath = os.path.join(cdir, key + ".neff")
            if os.path.exists(cpath):
                dst = os.path.join(tmpdir, neff_name)
                shutil.copyfile(cpath, dst)
                return dst
            out = _orig_compile(bir_json, tmpdir, neff_name)
            try:
                shutil.copyfile(out, cpath + ".tmp")
                os.replace(cpath + ".tmp", cpath)
            except OSError:
                pass
            return out
        except Exception:
            return _orig_compile(bir_json, tmpdir, neff_name)

    _bu.compile_bir_kernel = _cached_compile
    try:
        import concourse.bass2jax as _b2j
        _b2j.compile_bir_kernel = _cached_compile
    except ImportError:
        pass
    _bu._neff_cache_installed = True


class PB:

    SEMS = ("LD", "LD2", "ST", "V", "S", "T", "G", "X")

    def __init__(self, nc, block, semh):
        self.nc = nc
        self.block = block
        self.semh = semh
        self.cnt = {k: 0 for k in semh}

    def wait(self, eng, sem, val=None):
        v = self.cnt[sem] if val is None else val
        h = self.semh[sem]
        getattr(self.block, eng)(lambda e: e.wait_ge(h, v))
        return v

    def dma(self, eng, out, in_, sem="LD", **kw):
        self.cnt[sem] += 16
        h = self.semh[sem]
        def f(e, out=out, in_=in_, kw=kw):
            e.dma_start(out=out, in_=in_, **kw).then_inc(h, 16)
        getattr(self.block, eng)(f)
        return self.cnt[sem]

    def op(self, eng, fn, inc=None):
        if inc is None:
            getattr(self.block, eng)(fn)
            return None
        self.cnt[inc] += 1
        h = self.semh[inc]
        def f(e, fn=fn):
            fn(e).then_inc(h, 1)
        getattr(self.block, eng)(f)
        return self.cnt[inc]

    def opw(self, eng, fn, inc):
        c = self.op(eng, fn, inc=inc)
        self.wait(eng, inc, c)
        return c

    def phase_barrier(self, engines=ENGINES):
        for eng in engines:
            for sem in self.SEMS:
                if self.cnt[sem]:
                    self.wait(eng, sem)


@contextlib.contextmanager
def program(nc):
    with contextlib.ExitStack() as st:
        hold = st.enter_context
        semh = {k: hold(nc.semaphore("sem" + k)) for k in PB.SEMS}
        block = st.enter_context(nc.Block())
        yield PB(nc, block, semh), hold


def emit_dequant_tile(pb, w16, nib, sc1, sc2, plane_out, hi, sc_exp=None):
    V = "vector"
    n = nib[:]
    if hi:
        pb.op(V, lambda v: v.tensor_scalar(n, w16[:], 4, 15, ALU.logical_shift_right, ALU.bitwise_and))
    else:
        pb.op(V, lambda v: v.tensor_single_scalar(n, w16[:], 15, ALU.bitwise_and))
    a, b = sc1[:], sc2[:]
    pb.op(V, lambda v: v.tensor_single_scalar(a, n, 6, ALU.bitwise_and))
    pb.op(V, lambda v: v.tensor_single_scalar(a, a, 2, ALU.min))
    pb.op(V, lambda v: v.tensor_single_scalar(b, n, 1, ALU.bitwise_and))
    pb.op(V, lambda v: v.tensor_tensor(out=a, in0=a, in1=b, op=ALU.add))
    pb.op(V, lambda v: v.tensor_scalar(b, n, 6, 1, ALU.bitwise_and, ALU.logical_shift_right))
    pb.op(V, lambda v: v.tensor_scalar(b, b, 1, 125, ALU.max, ALU.add))
    pb.op(V, lambda v: v.tensor_single_scalar(b, b, 7, ALU.logical_shift_left))
    pb.op(V, lambda v: v.tensor_scalar(n, n, 8, 12, ALU.bitwise_and, ALU.logical_shift_left))
    pb.op(V, lambda v: v.tensor_tensor(out=b, in0=b, in1=n, op=ALU.bitwise_or))
    pb.op(V, lambda v: v.tensor_copy(plane_out[:], a))
    pb.opw(V, lambda v: v.tensor_tensor(out=plane_out[:], in0=plane_out[:],
                                        in1=b.bitcast(BF16), op=ALU.mult), inc="V")
    if sc_exp is not None:
        pb.opw(V, lambda v: v.tensor_tensor(out=plane_out[:], in0=plane_out[:],
                                            in1=sc_exp[:], op=ALU.mult), inc="V")


def emit_ln_tile(pb, xt, xn_out, stats, epsc, d_dim, out_dt_note="bf16"):
    s1, s2, mean, var, sv, rstd, sq = (stats[k] for k in
                                       ("s1", "s2", "mean", "var", "sv", "rstd", "sq"))
    iv = 1.0 / d_dim
    cv = pb.op("vector", lambda v: v.reduce_sum(s1[:], xt[:], axis=AX.X), inc="V")
    cs = pb.op("scalar", lambda s: s.activation(out=sq[:], in_=xt[:], func=AF.Square,
                                                accum_out=s2[:]), inc="S")
    pb.wait("scalar", "V", cv)
    cs2 = pb.op("scalar", lambda s: s.mul(mean[:], s1[:], iv), inc="S")
    pb.wait("vector", "S", cs2)
    pb.opw("vector", lambda v: v.tensor_tensor(out=var[:], in0=mean[:], in1=mean[:],
                                               op=ALU.mult), inc="V")
    cva = pb.opw("vector", lambda v: v.tensor_scalar(var[:], s2[:], iv, var[:],
                                                     ALU.mult, ALU.subtract), inc="V")
    pb.wait("scalar", "V", cva)
    css = pb.op("scalar", lambda s: s.activation(out=sv[:], in_=var[:], func=AF.Sqrt,
                                                 bias=epsc[:]), inc="S")
    pb.wait("vector", "S", css)
    pb.opw("vector", lambda v: v.reciprocal(rstd[:], sv[:]), inc="V")
    c = pb.op("vector", lambda v: v.tensor_scalar(xn_out[:], xt[:], mean[:], rstd[:],
                                                  ALU.subtract, ALU.mult), inc="V")
    return c


def build_ffn1(dbg=False):
    nc = bass.Bass("TRN2", target_bir_lowering=False)
    T_ALL, D, NS = 8192, 2048, 2048
    TC = 512
    NCH = T_ALL // TC
    x1 = nc.dram_tensor("x1", [T_ALL, D], F32, kind="ExternalInput").ap()
    gpk = nc.dram_tensor("gpk", [1024, NS], I32, kind="ExternalInput").ap()
    upk = nc.dram_tensor("upk", [1024, NS], I32, kind="ExternalInput").ap()
    dpk = nc.dram_tensor("dpk", [1024, D], I32, kind="ExternalInput").ap()
    gsc = nc.dram_tensor("gsc", [64, NS], F32, kind="ExternalInput").ap()
    usc = nc.dram_tensor("usc", [64, NS], F32, kind="ExternalInput").ap()
    dsc = nc.dram_tensor("dsc", [64, D], F32, kind="ExternalInput").ap()
    po = nc.dram_tensor("po", [D, T_ALL], F32, kind="ExternalOutput").ap()
    inter_d = nc.dram_tensor("inter_d", [NS // 128, NCH, 128, TC], BF16,
                             kind="ExternalOutput" if dbg else "Internal").ap()
    if dbg:
        dbg_g = nc.dram_tensor("dbg_g", [128, NS], F32, kind="ExternalOutput").ap()
        dbg_x = nc.dram_tensor("dbg_x", [128, 512], F32, kind="ExternalOutput").ap()
        dbg_d = nc.dram_tensor("dbg_d", [128, D], F32, kind="ExternalOutput").ap()
        dbg_ga = nc.dram_tensor("dbg_ga", [128, 512], F32, kind="ExternalOutput").ap()
        dbg_it = nc.dram_tensor("dbg_it", [128, 512], F32, kind="ExternalOutput").ap()

    with program(nc) as (pb, hold):
        stats = {k: hold(nc.sbuf_tensor("st_" + k, [128, 1], F32))
                 for k in ("s1", "s2", "mean", "var", "sv", "rstd")}
        stats["sq"] = None
        epsc = hold(nc.sbuf_tensor("epsc", [128, 1], F32))
        ident = hold(nc.sbuf_tensor("ident", [128, 128], BF16))
        pb.op("vector", lambda v: v.memset(epsc[:], 1e-5))
        identin = nc.dram_tensor("identin", [128, 128], F32, kind="ExternalInput").ap()
        pb.dma("gpsimd", ident[:], identin[:], "LD")
        pb.wait("tensor", "LD")

        def dequant_weight(scr, pk_ap, sc_ap, ncols, planes):
            w32, w16, nib, sc1, sc2, scex = scr
            """planes: list of 16 bf16 [128, ncols] tensors (tile t lo, tile t hi)."""
            for t in range(8):
                ld = pb.dma("sync", w32[:, 0:ncols], pk_ap[t * 128:(t + 1) * 128, :], "LD")
                for j in range(8):
                    r = 8 * t + j
                    pb.dma("gpsimd", scex[16 * j:16 * j + 16, 0:ncols],
                           sc_ap[r:r + 1, :].partition_broadcast(16)[:, 0, :], "LD2")
                pb.wait("vector", "LD")
                pb.wait("vector", "LD2")
                pb.opw("vector", lambda v, nc_=ncols: v.tensor_copy(w16[:, 0:nc_], w32[:, 0:nc_]), inc="V")
                emit_dequant_tile(pb, w16[:, 0:ncols], nib[:, 0:ncols], sc1[:, 0:ncols],
                                  sc2[:, 0:ncols], planes[2 * t][:, 0:ncols], False,
                                  scex[:, 0:ncols])
                emit_dequant_tile(pb, w16[:, 0:ncols], nib[:, 0:ncols], sc1[:, 0:ncols],
                                  sc2[:, 0:ncols], planes[2 * t + 1][:, 0:ncols], True,
                                  scex[:, 0:ncols])
                pb.wait("sync", "V")
                pb.wait("gpsimd", "V")

        with contextlib.ExitStack() as ph1:
            h1 = ph1.enter_context
            gpl = [h1(nc.sbuf_tensor(f"g{i}", [128, NS], BF16)) for i in range(16)]
            upl = [h1(nc.sbuf_tensor(f"u{i}", [128, NS], BF16)) for i in range(16)]
            with contextlib.ExitStack() as scs:
                hs = scs.enter_context
                scr = (hs(nc.sbuf_tensor("w32", [128, 2048], I32)),
                       hs(nc.sbuf_tensor("w16", [128, 2048], I16)),
                       hs(nc.sbuf_tensor("nib", [128, 2048], I16)),
                       hs(nc.sbuf_tensor("sc1", [128, 2048], I16)),
                       hs(nc.sbuf_tensor("sc2", [128, 2048], I16)),
                       hs(nc.sbuf_tensor("scex", [128, 2048], BF16)))
                dequant_weight(scr, gpk, gsc, NS, gpl)
                dequant_weight(scr, upk, usc, NS, upl)
            pb.phase_barrier()
            if dbg:
                pb.dma("gpsimd", dbg_g[:], gpl[0][:], "ST")

            xtile = h1(nc.sbuf_tensor("xtile", [128, D], F32))
            xnt = h1(nc.sbuf_tensor("xnt", [128, D], BF16))
            stats["sq"] = xnt
            xeo = [h1(nc.sbuf_tensor(f"xeo{i}", [128, TC], BF16)) for i in range(32)]
            tpp = h1(nc.psum_tensor("tpp", [128, TC], BF16))
            pg = h1(nc.psum_tensor("pg", [128, TC], F32))
            pu = h1(nc.psum_tensor("pu", [128, TC], F32))
            gsq = h1(nc.sbuf_tensor("gsq", [128, TC], BF16))
            gp2 = h1(nc.sbuf_tensor("gp2", [128, TC], BF16))
            gt3 = h1(nc.sbuf_tensor("gt3", [128, TC], BF16))
            gt4 = h1(nc.sbuf_tensor("gt4", [128, TC], BF16))
            itl = h1(nc.sbuf_tensor("itl", [128, TC], BF16))

            for tc in range(NCH):
                for tt in range(TC // 128):
                    trow = tc * TC + tt * 128
                    ldx = pb.dma("sync", xtile[:], x1[trow:trow + 128, :], "LD")
                    pb.wait("vector", "LD", ldx)
                    pb.wait("scalar", "LD", ldx)
                    pb.wait("scalar", "T")
                    pb.wait("vector", "T")
                    cxn = emit_ln_tile(pb, xtile, xnt, stats, epsc, D)
                    pb.wait("tensor", "V", cxn)
                    for et in range(8):
                        for par in (0, 1):
                            src = APClass(xnt, et * 256 + par, [[D, 128], [2, 128]])
                            ct = pb.op("tensor", lambda t, src=src: t.transpose(
                                tpp[:, 0:128], src, ident[:]), inc="T")
                            pb.wait("vector", "T", ct)
                            dst = xeo[2 * et + par]
                            cv = pb.op("vector", lambda v, dst=dst, tt=tt: v.tensor_copy(
                                dst[:, tt * 128:tt * 128 + 128], tpp[:, 0:128]), inc="V")
                            pb.wait("tensor", "V", cv)
                    pb.wait("sync", "V")
                pb.wait("tensor", "V")
                if dbg and tc == 0:
                    pb.wait("gpsimd", "V")
                    pb.dma("gpsimd", dbg_x[:], xeo[0][:], "ST")
                for nt in range(16):
                    nsl = slice(nt * 128, nt * 128 + 128)
                    for ik in range(16):
                        wtile = (gpl[ik])[:, nsl]
                        ct = pb.op("tensor", lambda t, w=wtile, ik=ik: t.matmul(
                            pg[:], w, xeo[ik][:], start=(ik == 0), stop=(ik == 15)),
                            inc="T" if ik == 15 else None)
                    for ik in range(16):
                        wtile = (upl[ik])[:, nsl]
                        ct2 = pb.op("tensor", lambda t, w=wtile, ik=ik: t.matmul(
                            pu[:], w, xeo[ik][:], start=(ik == 0), stop=(ik == 15)),
                            inc="T" if ik == 15 else None)
                    pb.wait("scalar", "T", ct2)
                    pb.wait("vector", "T", ct2)
                    cs = pb.op("scalar", lambda s: s.activation(out=gsq[:], in_=pg[:],
                                                                func=AF.Square), inc="S")
                    pb.wait("vector", "S", cs)
                    pb.opw("vector", lambda v: v.tensor_scalar(
                        gp2[:], gsq[:], 0.0356774081, 0.7978845608, ALU.mult, ALU.add), inc="V")
                    cv = pb.opw("vector", lambda v: v.tensor_tensor(
                        out=gt3[:], in0=pg[:], in1=gp2[:], op=ALU.mult), inc="V")
                    pb.wait("scalar", "V", cv)
                    cs2 = pb.op("scalar", lambda s: s.activation(out=gt4[:], in_=gt3[:],
                                                                 func=AF.Tanh), inc="S")
                    pb.wait("vector", "S", cs2)
                    pb.opw("vector", lambda v: v.tensor_scalar(
                        gt4[:], gt4[:], 1.0, 0.5, ALU.add, ALU.mult), inc="V")
                    pb.opw("vector", lambda v: v.tensor_tensor(
                        out=gt3[:], in0=pg[:], in1=gt4[:], op=ALU.mult), inc="V")
                    cfin = pb.opw("vector", lambda v: v.tensor_tensor(
                        out=itl[:], in0=pu[:], in1=gt3[:], op=ALU.mult), inc="V")
                    if dbg and tc == 5 and nt == 3:
                        cga = pb.op("vector", lambda v: v.tensor_copy(gsq[:], pg[:]), inc="V")
                        pb.wait("gpsimd", "V", cga)
                        pb.dma("gpsimd", dbg_ga[:], gsq[:], "ST")
                        pb.dma("gpsimd", dbg_it[:], itl[:], "ST")
                        pb.wait("vector", "ST")
                    pb.wait("tensor", "V")
                    if os.environ.get("FFN1_SERIAL"):
                        pb.wait("tensor", "S")
                        pb.wait("vector", "S")
                        pb.wait("scalar", "V")
                        pb.wait("sync", "T")
                        pb.wait("scalar", "T")
                        pb.wait("vector", "T")
                    pb.wait("sync", "V", cfin)
                    pb.dma("sync", inter_d[nt, tc], itl[:], "ST")
                    pb.wait("vector", "ST")

        pb.phase_barrier()
        with contextlib.ExitStack() as ph2:
            h2 = ph2.enter_context
            dpl = [h2(nc.sbuf_tensor(f"d{i}", [128, D], BF16)) for i in range(16)]
            with contextlib.ExitStack() as scs:
                hs = scs.enter_context
                scr = (hs(nc.sbuf_tensor("w32b", [128, 2048], I32)),
                       hs(nc.sbuf_tensor("w16b", [128, 2048], I16)),
                       hs(nc.sbuf_tensor("nibb", [128, 2048], I16)),
                       hs(nc.sbuf_tensor("sc1b", [128, 2048], I16)),
                       hs(nc.sbuf_tensor("sc2b", [128, 2048], I16)),
                       hs(nc.sbuf_tensor("scexb", [128, 2048], BF16)))
                dequant_weight(scr, dpk, dsc, D, dpl)
            pb.phase_barrier()
            if dbg:
                pb.dma("gpsimd", dbg_d[:], dpl[0][:], "ST")
            itin = [h2(nc.sbuf_tensor(f"it{i}", [128, TC], BF16)) for i in range(16)]
            pd = h2(nc.psum_tensor("pd", [128, TC], F32))
            oevac = h2(nc.sbuf_tensor("oevac", [128, TC], F32))

            for tc in range(NCH):
                for nt in range(16):
                    pb.dma("sync", itin[nt][:], inter_d[nt, tc], "LD")
                pb.wait("tensor", "LD")
                for dmt in range(16):
                    dsl = slice(dmt * 128, dmt * 128 + 128)
                    for nt in range(16):
                        dw = dpl[2 * nt] if nt < 8 else dpl[2 * (nt - 8) + 1]
                        ct = pb.op("tensor", lambda t, dw=dw, nt=nt, dsl=dsl: t.matmul(
                            pd[:], dw[:, dsl], itin[nt][:],
                            start=(nt == 0), stop=(nt == 15)),
                            inc="T" if nt == 15 else None)
                    pb.wait("vector", "T", ct)
                    cv = pb.op("vector", lambda v: v.tensor_copy(oevac[:], pd[:]), inc="V")
                    pb.wait("tensor", "V", cv)
                    pb.wait("sync", "V", cv)
                    pb.dma("sync", po[dmt * 128:dmt * 128 + 128, tc * TC:(tc + 1) * TC],
                           oevac[:], "ST")
                    pb.wait("vector", "ST")
                pb.wait("sync", "T")
        pb.wait("sync", "ST")
    return nc



def build_mamba(dbg=False):
    nc = bass.Bass("TRN2", target_bir_lowering=False)
    L, D, DH, NST, DTR = 2048, 2048, 1024, 16, 128
    LC = 512
    NLC = L // LC
    HDT = 8
    xb = nc.dram_tensor("xb", [L, D], F32, kind="ExternalInput").ap()
    w_in_x = nc.dram_tensor("w_in_x", [D, D], F32, kind="ExternalInput").ap()
    w_in_r = nc.dram_tensor("w_in_r", [D, DH], F32, kind="ExternalInput").ap()
    cw = [nc.dram_tensor(f"cw{k}", [D, D], F32, kind="ExternalInput").ap() for k in range(4)]
    cbT = nc.dram_tensor("cbT", [D, 1], F32, kind="ExternalInput").ap()
    w_x = nc.dram_tensor("w_x", [D, 160], F32, kind="ExternalInput").ap()
    w_dt_h = nc.dram_tensor("w_dt_h", [DTR, DH], F32, kind="ExternalInput").ap()
    b_dt_h = nc.dram_tensor("b_dt_h", [128, 8], F32, kind="ExternalInput").ap()
    a_log_h = nc.dram_tensor("a_log_h", [DH, NST], F32, kind="ExternalInput").ap()
    d_h = nc.dram_tensor("d_h", [128, 8], F32, kind="ExternalInput").ap()
    w_out_h = nc.dram_tensor("w_out_h", [DH, D], F32, kind="ExternalInput").ap()
    identin = nc.dram_tensor("identin2", [128, 128], F32, kind="ExternalInput").ap()
    mo = nc.dram_tensor("mo", [D, L], F32, kind="ExternalOutput").ap()
    bc_dram = nc.dram_tensor("bc_dram", [32, L], F32,
                             kind="ExternalOutput" if dbg else "Internal").ap()
    xs_dram = nc.dram_tensor("xs_dram", [16, 128, L], F32,
                             kind="ExternalOutput" if dbg else "Internal").ap()
    if dbg:
        dbg_xn = nc.dram_tensor("dbg_xn", [128, 2048], BF16, kind="ExternalOutput").ap()
        dbg_xc = nc.dram_tensor("dbg_xc", [128, 2048], BF16, kind="ExternalOutput").ap()
        dbg_dl = nc.dram_tensor("dbg_dl", [128, 512], F32, kind="ExternalOutput").ap()
        dbg_du = nc.dram_tensor("dbg_du", [128, 512], F32, kind="ExternalOutput").ap()
        dbg_y = nc.dram_tensor("dbg_y", [128, 512], F32, kind="ExternalOutput").ap()
        dbg_dbc = nc.dram_tensor("dbg_dbc", [128, 2048], F32, kind="ExternalOutput").ap()
        dbg_xna = nc.dram_tensor("dbg_xna", [128, 2048], BF16, kind="ExternalOutput").ap()
        dbg_xn1 = nc.dram_tensor("dbg_xn1", [128, 2048], BF16, kind="ExternalOutput").ap()
        dbg_tm = nc.dram_tensor("dbg_tm", [128, 512], F32, kind="ExternalOutput").ap()
        dbg_pd = nc.dram_tensor("dbg_pd", [128, 512], F32, kind="ExternalOutput").ap()
        dbg_dbc2 = nc.dram_tensor("dbg_dbc2", [128, 512], BF16, kind="ExternalOutput").ap()
        dbg_wdt = nc.dram_tensor("dbg_wdt", [128, 1024], BF16, kind="ExternalOutput").ap()

    with program(nc) as (pb, hold):
        stats = {k: hold(nc.sbuf_tensor("st_" + k, [128, 1], F32))
                 for k in ("s1", "s2", "mean", "var", "sv", "rstd")}
        epsc = hold(nc.sbuf_tensor("epsc", [128, 1], F32))
        onec = hold(nc.sbuf_tensor("onec", [128, 1], F32))
        ident = hold(nc.sbuf_tensor("ident", [128, 128], BF16))
        pb.op("vector", lambda v: v.memset(epsc[:], 1e-5))
        pb.op("vector", lambda v: v.memset(onec[:], 1.0))
        pb.dma("gpsimd", ident[:], identin[:], "LD")
        pb.wait("tensor", "LD")

        rT = hold(nc.sbuf_tensor("rT", [128, 8 * L], BF16))
        phABC = contextlib.ExitStack()
        xc = phABC.enter_context(nc.sbuf_tensor("xc", [128, 16 * (L + 8)], BF16))
        XCW = L + 8
        for i in range(16):
            pb.op("vector", lambda v, i=i: v.memset(xc[:, i * XCW:i * XCW + 8], 0.0))

        def load_wcol(wap, colt, dest, nk=16, sem="LD", width=128):
            K2 = wap.tensor.shape[1]
            src = APClass(wap.tensor, colt * width,
                          [[K2, 128], [128 * K2, nk], [1, width]])
            return pb.dma("gpsimd", dest, src, sem)

        with contextlib.ExitStack() as phA:
            hA = phA.enter_context
            xn = hA(nc.sbuf_tensor("xn", [128, 16 * L], BF16))
            xtile = hA(nc.sbuf_tensor("xtile", [128, D], F32))
            xnt = hA(nc.sbuf_tensor("xnt", [128, D], BF16))
            stats["sq"] = xnt
            tpp = hA(nc.psum_tensor("tpp", [128, 512], BF16))
            for tt in range(16):
                ldx = pb.dma("sync", xtile[:], xb[tt * 128:(tt + 1) * 128, :], "LD")
                pb.wait("vector", "LD", ldx)
                pb.wait("scalar", "LD", ldx)
                pb.wait("scalar", "T")
                pb.wait("vector", "T")
                cxn = emit_ln_tile(pb, xtile, xnt, stats, epsc, D)
                pb.wait("tensor", "V", cxn)
                for dt in range(16):
                    j = dt % 4
                    ct = pb.op("tensor", lambda t, dt=dt, j=j: t.transpose(
                        tpp[:, j * 128:(j + 1) * 128],
                        xnt[:, dt * 128:(dt + 1) * 128], ident[:]), inc="T")
                    pb.wait("vector", "T", ct)
                    cv = pb.op("vector", lambda v, dt=dt, j=j, tt=tt: v.tensor_copy(
                        xn[:, dt * L + tt * 128: dt * L + (tt + 1) * 128],
                        tpp[:, j * 128:(j + 1) * 128]), inc="V")
                    pb.wait("tensor", "V", cv)
                pb.wait("sync", "V")
            if dbg:
                pb.wait("gpsimd", "V")
                pb.wait("sync", "V")
                pb.dma("sync", dbg_xna[:], xn[:, 0:L], "ST")
                pb.wait("sync", "ST")
            wbuf = hA(nc.sbuf_tensor("wbuf", [128, 2 * D], BF16))
            pz = hA(nc.psum_tensor("pz", [128, 512], F32))
            for zt in range(24):
                wap = w_in_x if zt < 16 else w_in_r
                colt = zt if zt < 16 else zt - 16
                par = zt % 2
                lw = load_wcol(wap, colt, wbuf[:, par * D:(par + 1) * D])
                pb.wait("tensor", "LD", lw)
                for ch in range(4):
                    for kt in range(16):
                        cm = pb.op("tensor", lambda t, kt=kt, ch=ch, par=par: t.matmul(
                            pz[:], wbuf[:, par * D + kt * 128: par * D + (kt + 1) * 128],
                            xn[:, kt * L + ch * 512: kt * L + (ch + 1) * 512],
                            start=(kt == 0), stop=(kt == 15)), inc="T" if kt == 15 else None)
                    pb.wait("vector", "T", cm)
                    if zt < 16:
                        cv = pb.op("vector", lambda v, zt=zt, ch=ch: v.tensor_copy(
                            xc[:, zt * XCW + 8 + ch * 512: zt * XCW + 8 + (ch + 1) * 512],
                            pz[:]), inc="V")
                    else:
                        cv = pb.op("vector", lambda v, zt=zt, ch=ch: v.tensor_copy(
                            rT[:, (zt - 16) * L + ch * 512: (zt - 16) * L + (ch + 1) * 512],
                            pz[:]), inc="V")
                    pb.wait("tensor", "V", cv)
                pb.wait("gpsimd", "T")
            pb.phase_barrier()

        with contextlib.ExitStack() as phC:
            hC = phC.enter_context
            cwb = hC(nc.sbuf_tensor("cwb", [128, 8 * D], BF16))
            pcv = hC(nc.psum_tensor("pcv", [128, 512], F32))
            cbt = hC(nc.sbuf_tensor("cbt", [128, 1], F32))
            xse = hC(nc.sbuf_tensor("xse", [128, L], F32))
            for ot in range(16):
                par = ot % 2
                for k in range(4):
                    lw = load_wcol(cw[k], ot, cwb[:, (par * 4 + k) * D:(par * 4 + k + 1) * D])
                ldc = pb.dma("sync", cbt[:], cbT[ot * 128:(ot + 1) * 128, :], "LD")
                pb.wait("tensor", "LD", lw)
                pb.wait("tensor", "LD", ldc)
                for ch in range(4):
                    nmm = 0
                    for k in range(4):
                        for kt in range(16):
                            nmm += 1
                            cm = pb.op("tensor", lambda t, k=k, kt=kt, ch=ch, par=par, nmm=nmm: t.matmul(
                                pcv[:], cwb[:, (par * 4 + k) * D + kt * 128:(par * 4 + k) * D + (kt + 1) * 128],
                                xc[:, kt * XCW + 8 + ch * 512 - 3 + k: kt * XCW + 8 + ch * 512 - 3 + k + 512],
                                start=(nmm == 1), stop=(nmm == 64)),
                                inc="T" if nmm == 64 else None)
                    pb.wait("scalar", "T", cm)
                    pb.wait("scalar", "LD", ldc)
                    cs = pb.op("scalar", lambda s, ot=ot, ch=ch: s.add(
                        xse[:, ch * 512:(ch + 1) * 512], pcv[:], cbt[:]), inc="S")
                    pb.wait("tensor", "S", cs)
                pb.wait("sync", "S")
                pb.dma("sync", xs_dram[ot], xse[:], "ST")
                pb.wait("scalar", "ST")
                pb.wait("sync", "ST")
                pb.wait("gpsimd", "S")
            if dbg:
                pb.wait("gpsimd", "V")
                pb.wait("sync", "V")
                pb.dma("sync", dbg_xn[:], xn[:, 0:L], "ST")
                pb.dma("sync", dbg_xn1[:], xn[:, L:2*L], "ST")
                pb.dma("sync", dbg_xc[:], xc[:, 8:8+L], "ST")
                pb.wait("sync", "ST")
                pb.wait("gpsimd", "ST")
            pb.phase_barrier()
        phABC.close()

        dbcdt = hold(nc.sbuf_tensor("dbcdt", [128, L], F32))
        asb = hold(nc.sbuf_tensor("asb", [128, 8 * NST], F32))

        with contextlib.ExitStack() as phD:
            hD = phD.enter_context
            wxb = hD(nc.sbuf_tensor("wxb", [128, 16 * 160], BF16))
            src = APClass(w_x.tensor, 0, [[160, 128], [128 * 160, 16], [1, 160]])
            lw = pb.dma("gpsimd", wxb[:], src, "LD")
            usc = hD(nc.sbuf_tensor("usc", [128, 2 * 512], BF16))
            xsl = hD(nc.sbuf_tensor("xsl", [128, 2 * 512], F32))
            pdt = hD(nc.psum_tensor("pdt", [128, 512], F32))
            pbc = hD(nc.psum_tensor("pbc", [32, 512], F32))
            bcs = hD(nc.sbuf_tensor("bcs", [32, L], F32))
            pb.wait("tensor", "LD", lw)
            for ch in range(4):
                for kt in range(16):
                    par = kt % 2
                    lx = pb.dma("sync", xsl[:, par * 512:(par + 1) * 512],
                                xs_dram[kt][:, ch * 512:(ch + 1) * 512], "LD")
                    pb.wait("scalar", "LD", lx)
                    csi = pb.op("scalar", lambda s, par=par: s.activation(
                        out=usc[:, par * 512:(par + 1) * 512],
                        in_=xsl[:, par * 512:(par + 1) * 512], func=AF.Silu), inc="S")
                    pb.wait("tensor", "S", csi)
                    pb.op("tensor", lambda t, kt=kt, par=par: t.matmul(
                        pdt[:], wxb[:, kt * 160:kt * 160 + 128],
                        usc[:, par * 512:(par + 1) * 512],
                        start=(kt == 0), stop=(kt == 15)))
                    cm = pb.op("tensor", lambda t, kt=kt, par=par: t.matmul(
                        pbc[:], wxb[:, kt * 160 + 128:kt * 160 + 160],
                        usc[:, par * 512:(par + 1) * 512],
                        start=(kt == 0), stop=(kt == 15)), inc="T")
                    pb.wait("scalar", "T")
                    pb.wait("sync", "T")
                pb.wait("vector", "T", cm)
                cv1 = pb.op("vector", lambda v, ch=ch: v.tensor_copy(
                    dbcdt[:, ch * 512:(ch + 1) * 512], pdt[:]), inc="V")
                cv2 = pb.op("vector", lambda v, ch=ch: v.tensor_copy(
                    bcs[:, ch * 512:(ch + 1) * 512], pbc[:]), inc="V")
                pb.wait("tensor", "V", cv2)
            pb.wait("sync", "V")
            pb.dma("sync", bc_dram[:], bcs[:], "ST")
            aload = pb.dma("sync", asb[:],
                           APClass(a_log_h.tensor, 0, [[NST, 128], [128 * NST, 8], [1, NST]]),
                           "LD")
            pb.wait("scalar", "LD", aload)
            ca = pb.opw("scalar", lambda s: s.activation(out=asb[:], in_=asb[:], func=AF.Exp), inc="S")
            pb.wait("vector", "S", ca)
            can = pb.opw("vector", lambda v: v.tensor_single_scalar(asb[:], asb[:], -1.0, ALU.mult), inc="V")
            pb.wait("sync", "ST")
            if dbg:
                pb.wait("gpsimd", "V")
                pb.dma("gpsimd", dbg_dbc[:], dbcdt[:], "ST")
                pb.wait("gpsimd", "ST")
            pb.phase_barrier()

        with contextlib.ExitStack() as phE:
            hE = phE.enter_context
            wob = hE(nc.sbuf_tensor("wob", [128, 8 * D], BF16))
            wdtb = hE(nc.sbuf_tensor("wdtb", [128, DH], F32))
            bdtb = hE(nc.sbuf_tensor("bdtb", [128, 8], F32))
            dvb = hE(nc.sbuf_tensor("dvb", [128, 8], F32))
            lwo = pb.dma("gpsimd", wob[:],
                         APClass(w_out_h.tensor, 0, [[D, 128], [128 * D, 8], [1, D]]), "LD")
            lwdt = pb.dma("sync", wdtb[:], w_dt_h[:], "LD")
            pb.dma("sync", bdtb[:], b_dt_h[:], "LD")
            pb.dma("sync", dvb[:], d_h[:], "LD")
            delta = hE(nc.sbuf_tensor("delta", [128, 8 * LC], F32))
            du = hE(nc.sbuf_tensor("du", [128, 8 * LC], F32))
            yac = hE(nc.sbuf_tensor("yac", [128, 8 * LC], F32))
            omb = hE(nc.sbuf_tensor("omb", [128, 8 * LC], BF16))
            xsh = hE(nc.sbuf_tensor("xsh", [128, 8 * LC], F32))
            carry = hE(nc.sbuf_tensor("carry", [128, 128], F32))
            asc = hE(nc.sbuf_tensor("asc", [128, LC], F32))
            bsc = hE(nc.sbuf_tensor("bsc", [128, LC], F32))
            hsc = hE(nc.sbuf_tensor("hsc", [128, LC], F32))
            tmp = hE(nc.sbuf_tensor("tmp", [128, LC], F32))
            usc2 = hE(nc.sbuf_tensor("usc2", [128, LC], F32))
            oev = hE(nc.sbuf_tensor("oev", [128, LC], F32))
            bbc = hE(nc.sbuf_tensor("bbc", [128, 2 * LC], F32))
            cbc = hE(nc.sbuf_tensor("cbc", [128, 2 * LC], F32))
            pdl0 = hE(nc.psum_tensor("pdl", [128, LC], F32))
            pdl = pdl0 if not os.environ.get("MPROBE") else hE(nc.psum_tensor("pdlx", [128, LC], F32))
            pwo = hE(nc.psum_tensor("pwo", [128, LC], F32))
            pb.op("vector", lambda v: v.memset(carry[:], 0.0))
            pb.wait("tensor", "LD")
            pb.wait("scalar", "LD")
            for lc in range(NLC):
                for dt in range(HDT):
                    pb.dma("sync", xsh[:, dt * LC:(dt + 1) * LC],
                           xs_dram[dt][:, lc * LC:(lc + 1) * LC], "LD")
                pb.wait("scalar", "LD")
                pb.wait("vector", "LD")
                for dt in range(HDT):
                    dl = delta[:, dt * LC:(dt + 1) * LC]
                    duv = du[:, dt * LC:(dt + 1) * LC]
                    cm = pb.op("tensor", lambda t, dt=dt, lc=lc: t.matmul(
                        pdl[:], wdtb[:, dt * 128:(dt + 1) * 128],
                        dbcdt[:, lc * LC:(lc + 1) * LC], start=True, stop=True), inc="T")
                    if dbg and lc == 0 and dt == 0:
                        pb.wait("vector", "T", cm)
                        cdump = pb.op("vector", lambda v: v.tensor_copy(oev[:], pdl[:]), inc="V")
                        pb.wait("sync", "V", cdump)
                        pb.dma("sync", dbg_pd[:], oev[:], "ST")
                        pb.wait("scalar", "ST")
                        pb.wait("tensor", "V", cdump)
                    pb.wait("scalar", "T", cm)
                    cs = pb.opw("scalar", lambda s, dt=dt: s.add(
                        tmp[:], pdl[:], bdtb[:, dt:dt + 1]), inc="S")
                    if dbg and lc == 0 and dt == 0:
                        pb.wait("sync", "S", cs)
                        pb.dma("sync", dbg_tm[:], tmp[:], "ST")
                        pb.dma("sync", dbg_dbc2[:], dbcdt[:, 0:512], "ST")
                        pb.dma("sync", dbg_wdt[:], wdtb[:], "ST")
                        pb.wait("scalar", "ST")
                    pb.wait("tensor", "S", cs)
                    pb.wait("vector", "S", cs)
                    cv = pb.op("vector", lambda v, dl=dl: v.tensor_relu(dl, tmp[:]), inc="V")
                    cs = pb.opw("scalar", lambda s: s.activation(out=usc2[:], in_=tmp[:], func=AF.Abs), inc="S")
                    cs = pb.opw("scalar", lambda s: s.activation(out=usc2[:], in_=usc2[:], func=AF.Exp, scale=-1.0), inc="S")
                    cs = pb.opw("scalar", lambda s: s.activation(out=usc2[:], in_=usc2[:], func=AF.Ln, bias=onec[:]), inc="S")
                    pb.wait("vector", "S", cs)
                    pb.wait("vector", "V", cv)
                    cdl = pb.opw("vector", lambda v, dl=dl: v.tensor_tensor(
                        out=dl, in0=dl, in1=usc2[:], op=ALU.add), inc="V")
                    cu = pb.op("scalar", lambda s, dt=dt: s.activation(
                        out=usc2[:], in_=xsh[:, dt * LC:(dt + 1) * LC], func=AF.Silu), inc="S")
                    pb.wait("vector", "S", cu)
                    cdu = pb.opw("vector", lambda v, dl=dl, duv=duv: v.tensor_tensor(
                        out=duv, in0=dl, in1=usc2[:], op=ALU.mult), inc="V")
                    cz = pb.op("vector", lambda v, dt=dt: v.memset(yac[:, dt * LC:(dt + 1) * LC], 0.0), inc="V")
                    pb.wait("scalar", "V", cz)
                if dbg and lc == 0:
                    pb.wait("gpsimd", "V")
                    pb.dma("gpsimd", dbg_dl[:], delta[:, 0:LC], "ST")
                    pb.dma("gpsimd", dbg_du[:], du[:, 0:LC], "ST")
                    pb.wait("gpsimd", "ST")
                for n in range(NST):
                    par = n % 2
                    lb = pb.dma("gpsimd", bbc[:, par * LC:(par + 1) * LC],
                                APClass(bc_dram.tensor, n * L + lc * LC, [[0, 128], [1, LC]]),
                                "LD2")
                    lb2 = pb.dma("gpsimd", cbc[:, par * LC:(par + 1) * LC],
                                 APClass(bc_dram.tensor, (16 + n) * L + lc * LC, [[0, 128], [1, LC]]),
                                 "LD2")
                    pb.wait("vector", "LD2", lb2)
                    for dt in range(HDT):
                        cs = pb.op("scalar", lambda s, dt=dt, n=n: s.activation(
                            out=asc[:], in_=delta[:, dt * LC:(dt + 1) * LC], func=AF.Exp,
                            scale=asb[:, dt * NST + n: dt * NST + n + 1]), inc="S")
                        cb = pb.op("vector", lambda v, dt=dt, par=par: v.tensor_tensor(
                            out=bsc[:], in0=du[:, dt * LC:(dt + 1) * LC],
                            in1=bbc[:, par * LC:(par + 1) * LC], op=ALU.mult), inc="V")
                        pb.wait("vector", "S", cs)
                        pb.wait("vector", "V", cb)
                        chs = pb.opw("vector", lambda v, n=n, dt=dt, lc=lc: v.tensor_tensor_scan(
                            out=hsc[:], data0=asc[:], data1=bsc[:],
                            initial=(0.0 if lc == 0 else carry[:, n * 8 + dt: n * 8 + dt + 1]),
                            op0=ALU.mult, op1=ALU.add), inc="V")
                        cca = pb.op("vector", lambda v, n=n, dt=dt: v.tensor_copy(
                            carry[:, n * 8 + dt: n * 8 + dt + 1], hsc[:, LC - 1:LC]), inc="V")
                        ctm = pb.opw("vector", lambda v, par=par: v.tensor_tensor(
                            out=tmp[:], in0=hsc[:], in1=cbc[:, par * LC:(par + 1) * LC],
                            op=ALU.mult), inc="V")
                        cya = pb.opw("vector", lambda v, dt=dt: v.tensor_tensor(
                            out=yac[:, dt * LC:(dt + 1) * LC],
                            in0=yac[:, dt * LC:(dt + 1) * LC], in1=tmp[:], op=ALU.add), inc="V")
                        pb.wait("scalar", "V", cya)
                    pb.wait("gpsimd", "V")
                if dbg and lc == 0:
                    pb.wait("gpsimd", "V")
                    pb.dma("gpsimd", dbg_y[:], yac[:, 0:LC], "ST")
                    pb.wait("gpsimd", "ST")
                for dt in range(HDT):
                    cu = pb.op("scalar", lambda s, dt=dt: s.activation(
                        out=usc2[:], in_=xsh[:, dt * LC:(dt + 1) * LC], func=AF.Silu), inc="S")
                    pb.wait("vector", "S", cu)
                    pb.opw("vector", lambda v, dt=dt: v.tensor_scalar(
                        tmp[:], usc2[:], dvb[:, dt:dt + 1], None, ALU.mult), inc="V")
                    pb.opw("vector", lambda v, dt=dt: v.tensor_tensor(
                        out=yac[:, dt * LC:(dt + 1) * LC],
                        in0=yac[:, dt * LC:(dt + 1) * LC], in1=tmp[:], op=ALU.add), inc="V")
                    cr = pb.op("scalar", lambda s, dt=dt, lc=lc: s.activation(
                        out=usc2[:], in_=rT[:, dt * L + lc * LC: dt * L + (lc + 1) * LC],
                        func=AF.Silu), inc="S")
                    pb.wait("vector", "S", cr)
                    com = pb.opw("vector", lambda v, dt=dt: v.tensor_tensor(
                        out=omb[:, dt * LC:(dt + 1) * LC],
                        in0=yac[:, dt * LC:(dt + 1) * LC], in1=usc2[:], op=ALU.mult), inc="V")
                    pb.wait("scalar", "V", com)
                pb.wait("tensor", "V")
                for mt in range(16):
                    for dt in range(HDT):
                        cm = pb.op("tensor", lambda t, mt=mt, dt=dt: t.matmul(
                            pwo[:], wob[:, dt * D + mt * 128: dt * D + (mt + 1) * 128],
                            omb[:, dt * LC:(dt + 1) * LC],
                            start=(dt == 0), stop=(dt == HDT - 1)),
                            inc="T" if dt == HDT - 1 else None)
                    pb.wait("vector", "T", cm)
                    if mt < 8:
                        cv = pb.op("vector", lambda v, mt=mt: v.tensor_tensor(
                            out=oev[:], in0=pwo[:], in1=xsh[:, mt * LC:(mt + 1) * LC],
                            op=ALU.add), inc="V")
                    else:
                        cv = pb.op("vector", lambda v: v.tensor_copy(oev[:], pwo[:]), inc="V")
                    pb.wait("tensor", "V", cv)
                    pb.wait("sync", "V", cv)
                    pb.dma("sync", mo[mt * 128:(mt + 1) * 128, lc * LC:(lc + 1) * LC],
                           oev[:], "ST")
                    pb.wait("vector", "ST")
                pb.wait("scalar", "V")
                pb.wait("sync", "V")
        pb.wait("sync", "ST")
        pb.wait("gpsimd", "ST")
    return nc



def build_l0(dbg=False):
    nc = bass.Bass("TRN2", target_bir_lowering=False)
    S, D, NH, DKV, HM = 2048, 2048, 32, 64, 8192
    TO = 1024
    x0r = nc.dram_tensor("x0r", [S, D], F32, kind="ExternalInput").ap()
    wq = nc.dram_tensor("wq", [D, D], F32, kind="ExternalInput").ap()
    wk = nc.dram_tensor("wk", [D, D], F32, kind="ExternalInput").ap()
    wv = nc.dram_tensor("wv", [D, D], F32, kind="ExternalInput").ap()
    wo = nc.dram_tensor("wo", [D, D], F32, kind="ExternalInput").ap()
    w1pt = nc.dram_tensor("w1pt", [D, HM], F32, kind="ExternalInput").ap()
    w3pt = nc.dram_tensor("w3pt", [D, HM], F32, kind="ExternalInput").ap()
    w2t = nc.dram_tensor("w2t", [HM, D], F32, kind="ExternalInput").ap()
    relb = nc.dram_tensor("relb", [32, 32], F32, kind="ExternalInput").ap()
    relb01 = nc.dram_tensor("relb01", [2, 32], F32, kind="ExternalInput").ap()
    amap = nc.dram_tensor("amap", [2, 128], F32, kind="ExternalInput").ap()
    ind0 = nc.dram_tensor("ind0", [32, 16384], F32, kind="ExternalInput").ap()
    ind1 = nc.dram_tensor("ind1", [32, 16384], F32, kind="ExternalInput").ap()
    indw = nc.dram_tensor("indw", [32, 16384], F32, kind="ExternalInput").ap()
    qpat = nc.dram_tensor("qpat", [16, 1024], BF16, kind="ExternalInput").ap()
    kpat16 = nc.dram_tensor("kpat16", [16, 2048], BF16, kind="ExternalInput").ap()
    onepair = nc.dram_tensor("onepair", [128, 2], F32, kind="ExternalInput").ap()
    identin = nc.dram_tensor("identin3", [128, 128], F32, kind="ExternalInput").ap()
    h128in = nc.dram_tensor("h128in", [128, 128], F32, kind="ExternalInput").ap()
    h64in = nc.dram_tensor("h64in", [128, 128], F32, kind="ExternalInput").ap()
    y = nc.dram_tensor("y", [TO, D], F32, kind="ExternalOutput").ap()

    bm_dram = nc.dram_tensor("bm_dram", [32, 128], F32, kind="Internal").ap()
    dt_dram = nc.dram_tensor("dt_dram", [3, 32, 16384], BF16, kind="Internal").ap()
    w2s_dram = nc.dram_tensor("w2s_dram", [64, 128, D], BF16, kind="Internal").ap()
    attn_dram = nc.dram_tensor("attn_dram", [16, 128, TO], BF16,
                               kind="ExternalOutput" if dbg else "Internal").ap()
    if dbg:
        dbg_xn = nc.dram_tensor("dbg_xn", [128, 2048], BF16, kind="ExternalOutput").ap()
        dbg_wo = nc.dram_tensor("dbg_wo", [128, 1024], BF16, kind="ExternalOutput").ap()
        dbg_ff = nc.dram_tensor("dbg_ff", [128, 1024], BF16, kind="ExternalOutput").ap()
        dbg_hb = nc.dram_tensor("dbg_hb", [128, 1024], BF16, kind="ExternalOutput").ap()
        dbg_w2 = nc.dram_tensor("dbg_w2", [128, 128], BF16, kind="ExternalOutput").ap()
        dbg_ka = nc.dram_tensor("dbg_ka", [81, 2048], BF16, kind="ExternalOutput").ap()
        dbg_qa = nc.dram_tensor("dbg_qa", [81, 1024], BF16, kind="ExternalOutput").ap()
        dbg_wo2 = nc.dram_tensor("dbg_wo2", [128, 2048], BF16, kind="ExternalOutput").ap()
        dbg_pr = nc.dram_tensor("dbg_pr", [128, 32], BF16, kind="ExternalOutput").ap()
    w2p_dram = nc.dram_tensor("w2p_dram", [16, 64, 128, 128], BF16, kind="Internal").ap()

    with program(nc) as (pb, hold):
        stats = {k: hold(nc.sbuf_tensor("st_" + k, [128, 1], F32))
                 for k in ("s1", "s2", "mean", "var", "sv", "rstd")}
        epsc = hold(nc.sbuf_tensor("epsc", [128, 1], F32))
        ident = hold(nc.sbuf_tensor("ident", [128, 128], BF16))
        identf = hold(nc.sbuf_tensor("identf", [128, 128], F32))
        h128 = hold(nc.sbuf_tensor("h128", [128, 128], BF16))
        h64b = hold(nc.sbuf_tensor("h64b", [128, 128], BF16))
        onep = hold(nc.sbuf_tensor("onep", [128, 2], BF16))
        ones64 = hold(nc.sbuf_tensor("ones64", [1, 64], BF16))
        wfl = hold(nc.sbuf_tensor("wfl", [128, 2048], F32))
        pb.op("vector", lambda v: v.memset(epsc[:], 1e-5))
        pb.op("vector", lambda v: v.memset(ones64[:], 1.0))
        l1 = pb.dma("sync", identf[:], identin[:], "LD")
        pb.wait("vector", "LD", l1)
        pb.opw("vector", lambda v: v.tensor_copy(ident[:], identf[:]), inc="V")
        l2 = pb.dma("sync", wfl[:, 0:128], h128in[:], "LD")
        l3 = pb.dma("sync", wfl[:, 128:256], h64in[:], "LD")
        l4 = pb.dma("sync", wfl[:, 256:258], onepair[:], "LD")
        pb.wait("vector", "LD", l4)
        pb.opw("vector", lambda v: v.tensor_copy(h128[:], wfl[:, 0:128]), inc="V")
        pb.opw("vector", lambda v: v.tensor_copy(h64b[:], wfl[:, 128:256]), inc="V")
        pb.opw("vector", lambda v: v.tensor_copy(onep[:], wfl[:, 256:258]), inc="V")
        pb.wait("tensor", "V")
        pb.wait("sync", "V")

        def wcol3(wap, colt, dest, nk=16, width=128, sem="LD"):
            K2 = wap.tensor.shape[1]
            src = APClass(wap.tensor, colt * width,
                          [[K2, 128], [128 * K2, nk], [1, width]])
            return pb.dma("gpsimd", dest, src, sem)

        with contextlib.ExitStack() as phB:
            hB = phB.enter_context
            rb = hB(nc.sbuf_tensor("rb", [32, 32], F32))
            rb01 = hB(nc.sbuf_tensor("rb01", [2, 32], F32))
            am = hB(nc.sbuf_tensor("am", [2, 128], F32))
            indc = hB(nc.sbuf_tensor("indc", [32, 2048], F32))
            dsc = hB(nc.sbuf_tensor("dsc", [32, 2048], BF16))
            pbm = hB(nc.psum_tensor("pbm", [32, 512], F32))
            la = pb.dma("sync", rb[:], relb[:], "LD")
            lb = pb.dma("sync", rb01[:], relb01[:], "LD")
            lcm = pb.dma("sync", am[:], amap[:], "LD")
            pb.wait("tensor", "LD", lcm)
            cm = pb.op("tensor", lambda t: t.matmul(pbm[:, 0:128], rb01[:], am[:],
                                                    start=True, stop=True), inc="T")
            pb.wait("vector", "T", cm)
            cv = pb.op("vector", lambda v: v.tensor_copy(indc[:, 0:128], pbm[:, 0:128]), inc="V")
            pb.wait("sync", "V", cv)
            pb.dma("sync", bm_dram[:], indc[:, 0:128], "ST")
            pb.wait("tensor", "ST")
            for bi, iap in enumerate((ind0, ind1, indw)):
                for ch in range(8):
                    lic = pb.dma("sync", indc[:], iap[:, ch * 2048:(ch + 1) * 2048], "LD")
                    pb.wait("tensor", "LD", lic)
                    for q4 in range(4):
                        cm = pb.op("tensor", lambda t, q4=q4: t.matmul(
                            pbm[:], rb[:], indc[:, q4 * 512:(q4 + 1) * 512],
                            start=True, stop=True), inc="T")
                        pb.wait("vector", "T", cm)
                        cv = pb.op("vector", lambda v, q4=q4: v.tensor_copy(
                            dsc[:, q4 * 512:(q4 + 1) * 512], pbm[:]), inc="V")
                        pb.wait("tensor", "V", cv)
                    pb.wait("sync", "V")
                    pb.dma("sync", dt_dram[bi][:, ch * 2048:(ch + 1) * 2048], dsc[:], "ST")
                    pb.wait("sync", "ST")
                pb.wait("tensor", "ST")

        xn = hold(nc.sbuf_tensor("xn", [128, 16 * S], BF16))

        with contextlib.ExitStack() as phL:
            hL = phL.enter_context
            xtile = hL(nc.sbuf_tensor("xtile", [128, D], F32))
            xnt = hL(nc.sbuf_tensor("xnt", [128, D], BF16))
            stats["sq"] = xnt
            tpp = hL(nc.psum_tensor("tpp", [128, 512], BF16))
            for tt in range(16):
                ldx = pb.dma("sync", xtile[:], x0r[tt * 128:(tt + 1) * 128, :], "LD")
                pb.wait("vector", "LD", ldx)
                pb.wait("scalar", "LD", ldx)
                pb.wait("scalar", "T")
                pb.wait("vector", "T")
                cxn = emit_ln_tile(pb, xtile, xnt, stats, epsc, D)
                pb.wait("tensor", "V", cxn)
                for dt in range(16):
                    j = dt % 4
                    ct = pb.op("tensor", lambda t, dt=dt, j=j: t.transpose(
                        tpp[:, j * 128:(j + 1) * 128],
                        xnt[:, dt * 128:(dt + 1) * 128], ident[:]), inc="T")
                    pb.wait("vector", "T", ct)
                    cv = pb.op("vector", lambda v, dt=dt, j=j, tt=tt: v.tensor_copy(
                        xn[:, dt * S + tt * 128: dt * S + (tt + 1) * 128],
                        tpp[:, j * 128:(j + 1) * 128]), inc="V")
                    pb.wait("tensor", "V", cv)
                pb.wait("sync", "V")
            if dbg:
                pb.wait("sync", "V")
                pb.dma("sync", dbg_xn[:], xn[:, 0:S], "ST")
                pb.wait("sync", "ST")
            pb.phase_barrier()

        acc2 = hold(nc.sbuf_tensor("acc2", [128, 16 * TO], BF16))
        with contextlib.ExitStack() as phT:
            hT = phT.enter_context
            kT = hT(nc.sbuf_tensor("kT", [128, 2 * S], BF16))
            vT = hT(nc.sbuf_tensor("vT", [128, 2 * S], BF16))
            qT = hT(nc.sbuf_tensor("qT", [128, 2 * TO], BF16))
            kaug = [hT(nc.sbuf_tensor(f"kaug{i}", [81, S], BF16)) for i in range(4)]
            qaug = [hT(nc.sbuf_tensor(f"qaug{i}", [81, TO], BF16)) for i in range(4)]
            vaug = [hT(nc.sbuf_tensor(f"vaug{i}", [128, 16 * 65], BF16)) for i in range(4)]
            d0t = [hT(nc.sbuf_tensor(f"d0t{i}", [128, 3 * 128], BF16)) for i in range(4)]
            bmt = hT(nc.sbuf_tensor("bmt", [16, TO], BF16))
            bm3 = hT(nc.sbuf_tensor("bm3", [16, 512], F32))
            qp = hT(nc.sbuf_tensor("qp", [16, TO], BF16))
            bm3b = hT(nc.sbuf_tensor("bm3b", [16, 512], BF16))
            qqs = hT(nc.sbuf_tensor("qqs", [2, TO], F32))
            kks = hT(nc.sbuf_tensor("kks", [2, 512], F32))
            kkc = hT(nc.sbuf_tensor("kkc", [2, 1], F32))
            kkm = hT(nc.sbuf_tensor("kkm", [2, 1], F32))
            mrowb = hT(nc.sbuf_tensor("mrowb", [2, TO], BF16))
            atile = hT(nc.sbuf_tensor("atile", [128, 4 * 512], BF16))
            sqs = atile
            zr = hT(nc.sbuf_tensor("zr", [1, 512], F32))
            zrb = hT(nc.sbuf_tensor("zrb", [1, 512], BF16))
            zbb = hT(nc.sbuf_tensor("zbb", [64, 512], BF16))
            nsb = hT(nc.sbuf_tensor("nsb", [64, 512], BF16))
            osc = hT(nc.sbuf_tensor("osc", [64, 512], BF16))
            wbg = hT(nc.sbuf_tensor("wbg", [128, 2048], BF16))
            sp = hT(nc.psum_tensor("sp", [128, 4 * 512], F32))
            pnum = hT(nc.psum_tensor("pnum", [65, 512], F32))
            pzb = hT(nc.psum_tensor("pzb", [64, 512], F32))
            pqq = hT(nc.psum_tensor("pqq", [2, 512], F32))
            ptr = hT(nc.psum_tensor("ptr", [128, 512], BF16))

            pb.op("vector", lambda v: v.memset(bm3[:], 0.0))
            pb.wait("sync", "V")
            lq = pb.dma("sync", bm3[0:8, :], APClass(bm_dram.tensor, 0,
                        [[16, 8], [128, 32], [1, 16]]), "LD")
            pb.wait("vector", "LD", lq)
            pb.opw("vector", lambda v: v.tensor_copy(bm3b[:], bm3[:]), inc="V")
            pb.dma("sync", qp[:], qpat[:], "LD")
            for g in range(4):
                pb.op("vector", lambda v, g=g: v.memset(kaug[g][64:65, :], 1.0))
                lk = pb.dma("sync", kaug[g][65:81, :], kpat16[:], "LD")
                pb.op("vector", lambda v, g=g: v.memset(
                    APClass(vaug[g], 64, [[16 * 65, 128], [65, 16]]), 1.0))
            pb.wait("tensor", "LD")

            for grp in range(8):
                for which, wap, dest, ntok in (("q", wq, qT, TO), ("k", wk, kT, S), ("v", wv, vT, S)):
                    for sub in range(2):
                        colt = grp * 2 + sub
                        pb.wait("gpsimd", "T")
                        lw = wcol3(wap, colt, wbg[:])
                        pb.wait("tensor", "LD", lw)
                        nch = ntok // 512
                        for ch in range(nch):
                            for kt in range(16):
                                cmm = pb.op("tensor", lambda t, kt=kt, ch=ch: t.matmul(
                                    sp[:, 0:512], wbg[:, kt * 128:(kt + 1) * 128],
                                    xn[:, kt * S + ch * 512: kt * S + (ch + 1) * 512],
                                    start=(kt == 0), stop=(kt == 15)), inc="T" if kt == 15 else None)
                            pb.wait("vector", "T", cmm)
                            cv = pb.op("vector", lambda v, dest=dest, sub=sub, ch=ch: v.tensor_copy(
                                dest[:, sub * (dest.shape[1] // 2) + ch * 512:
                                     sub * (dest.shape[1] // 2) + (ch + 1) * 512], sp[:, 0:512]), inc="V")
                            pb.wait("tensor", "V", cv)
                pb.wait("scalar", "V")
                for sub in range(2):
                    cq = pb.op("scalar", lambda s, sub=sub: s.activation(
                        out=sqs[:, 0:TO], in_=qT[:, sub * TO:(sub + 1) * TO], func=AF.Square), inc="S")
                    pb.wait("tensor", "S", cq)
                    for ch in range(2):
                        cm = pb.op("tensor", lambda t, ch=ch: t.matmul(
                            pqq[:], onep[:], sqs[:, ch * 512:(ch + 1) * 512],
                            start=True, stop=True), inc="T")
                        pb.wait("vector", "T", cm)
                        cv = pb.opw("vector", lambda v, ch=ch: v.tensor_copy(
                            qqs[:, ch * 512:(ch + 1) * 512], pqq[:]), inc="V")
                        pb.wait("tensor", "V", cv)
                    pb.wait("scalar", "V")
                    ck = pb.op("scalar", lambda s, sub=sub: s.activation(
                        out=sqs[:], in_=kT[:, sub * S:(sub + 1) * S], func=AF.Square), inc="S")
                    pb.wait("tensor", "S", ck)
                    for ch in range(4):
                        cm = pb.op("tensor", lambda t, ch=ch: t.matmul(
                            pqq[:], onep[:], sqs[:, ch * 512:(ch + 1) * 512],
                            start=True, stop=True), inc="T")
                        pb.wait("vector", "T", cm)
                        cv = pb.opw("vector", lambda v: v.tensor_copy(kks[:], pqq[:]), inc="V")
                        pb.wait("tensor", "V", cv)
                        cv = pb.opw("vector", lambda v, ch=ch: v.reduce_max(
                            kkc[:] if ch else kkm[:], kks[:], axis=AX.X), inc="V")
                        if ch:
                            cv = pb.opw("vector", lambda v: v.tensor_tensor(
                                out=kkm[:], in0=kkm[:], in1=kkc[:], op=ALU.max), inc="V")
                    pb.wait("scalar", "V", cv)
                    csq = pb.op("scalar", lambda s: s.activation(out=kkm[:], in_=kkm[:], func=AF.Sqrt), inc="S")
                    pb.wait("scalar", "S", csq)
                    cs2 = pb.op("scalar", lambda s: s.activation(out=qqs[:], in_=qqs[:], func=AF.Sqrt), inc="S")
                    pb.wait("vector", "S", cs2)
                    cmr = pb.opw("vector", lambda v: v.tensor_scalar(
                        qqs[:], qqs[:], kkm[:], -40.0, ALU.mult, ALU.add), inc="V")
                    cmb = pb.opw("vector", lambda v: v.tensor_single_scalar(
                        mrowb[:], qqs[:], -1.0, ALU.mult), inc="V")
                    pb.wait("sync", "V", cmb)
                    for hh in range(2):
                        h_loc = sub * 2 + hh
                        pb.dma("sync", qaug[h_loc][64:65, :], mrowb[hh:hh + 1, :], "X")
                pb.wait("sync", "V")
                for h_loc in range(4):
                    sub, hh = divmod(h_loc, 2)
                    pb.dma("sync", kaug[h_loc][0:64, :], kT[:, sub * S:(sub + 1) * S].bitcast(BF16) if False else
                           APClass(kT, sub * S * 1 + hh * 64 * (2 * S), [[2 * S, 64], [1, S]]), "X")
                    pb.dma("sync", qaug[h_loc][0:64, :],
                           APClass(qT, sub * TO + hh * 64 * (2 * TO), [[2 * TO, 64], [1, TO]]), "X")
                    hglob = grp * 4 + h_loc
                    for ch in range(2):
                        cmb2 = pb.op("tensor", lambda t, hglob=hglob, ch=ch: t.matmul(
                            pzb[0:16, :], bm3b[:, hglob * 16:(hglob + 1) * 16],
                            qp[:, ch * 512:(ch + 1) * 512],
                            start=True, stop=True), inc="T")
                        pb.wait("vector", "T", cmb2)
                        cvb = pb.op("vector", lambda v, ch=ch: v.tensor_copy(
                            bmt[0:16, ch * 512:(ch + 1) * 512], pzb[0:16, :]), inc="V")
                        pb.wait("tensor", "V", cvb)
                    pb.wait("sync", "V")
                    pb.dma("sync", qaug[h_loc][65:81, :], bmt[0:16, :], "X")
                    pb.wait("vector", "X")
                    for bi in range(3):
                        pb.dma("sync", d0t[h_loc][:, bi * 128:(bi + 1) * 128],
                               APClass(dt_dram.tensor, bi * 32 * 16384 + hglob * 16384,
                                       [[128, 128], [1, 128]]), "X")
                    for k0 in range(16):
                        ct = pb.op("tensor", lambda t, sub=sub, hh=hh, k0=k0: t.transpose(
                            ptr[:, 0:64],
                            APClass(vT, sub * S + hh * 64 * (2 * S) + k0 * 128,
                                    [[2 * S, 64], [1, 128]]),
                            ident[64 * hh:64 * (hh + 1), 64 * hh:64 * (hh + 1)]), inc="T")
                        pb.wait("vector", "T", ct)
                        cv = pb.op("vector", lambda v, h_loc=h_loc, k0=k0: v.tensor_copy(
                            vaug[h_loc][:, k0 * 65:k0 * 65 + 64], ptr[:, 0:64]), inc="V")
                        pb.wait("tensor", "V", cv)
                pb.wait("tensor", "X")
                pb.wait("tensor", "V")
                if dbg and grp == 0:
                    pb.wait("sync", "X")
                    pb.wait("sync", "V")
                    pb.dma("sync", dbg_ka[:], kaug[0][:], "ST")
                    pb.dma("sync", dbg_qa[:], qaug[0][:], "ST")
                    pb.wait("sync", "ST")
                for h_loc in range(4):
                    hglob = grp * 4 + h_loc
                    for qc in range(2):
                        for kq in range(4):
                            nk6 = 4
                            for k6 in range(nk6):
                                k0 = kq * 4 + k6
                                cmm = pb.op("tensor", lambda t, h_loc=h_loc, k0=k0, k6=k6, qc=qc: t.matmul(
                                    sp[:, k6 * 512:(k6 + 1) * 512],
                                    kaug[h_loc][:, k0 * 128:(k0 + 1) * 128],
                                    qaug[h_loc][:, qc * 512:(qc + 1) * 512],
                                    start=True, stop=True), inc="T")
                                for q0g in range(4):
                                    q0 = qc * 4 + q0g
                                    band = None
                                    if k0 == q0:
                                        band = 0
                                    elif k0 == (q0 - 1) % 16:
                                        band = 1 if q0 != 0 else 2
                                    if band is not None:
                                        cmm = pb.op("tensor", lambda t, band=band, h_loc=h_loc, k6=k6, q0g=q0g: t.matmul(
                                            sp[:, k6 * 512 + q0g * 128: k6 * 512 + (q0g + 1) * 128],
                                            ident[:], d0t[h_loc][:, band * 128:(band + 1) * 128],
                                            start=False, stop=True), inc="T")
                            pb.wait("scalar", "T", cmm)
                            cex = pb.op("scalar", lambda s, n=nk6: s.activation(
                                out=atile[:, 0:n * 512],
                                in_=sp[:, 0:n * 512], func=AF.Exp), inc="S")
                            pb.wait("tensor", "S", cex)
                            for k6 in range(nk6):
                                k0 = kq * 4 + k6
                                cmv = pb.op("tensor", lambda t, h_loc=h_loc, k0=k0, k6=k6: t.matmul(
                                    pnum[:], vaug[h_loc][:, k0 * 65:(k0 + 1) * 65],
                                    atile[:, k6 * 512:(k6 + 1) * 512],
                                    start=(k0 == 0), stop=(k0 == 15)), inc="T" if k0 == 15 else None)
                            pb.wait("scalar", "T")
                        pb.wait("vector", "T")
                        cr1 = pb.opw("vector", lambda v: v.reciprocal(zr[:], pnum[64:65, :]), inc="V")
                        cr2 = pb.opw("vector", lambda v: v.tensor_copy(zrb[:], zr[:]), inc="V")
                        pb.wait("tensor", "V", cr2)
                        cbz = pb.op("tensor", lambda t: t.matmul(pzb[:], ones64[:], zrb[:],
                                                                 start=True, stop=True), inc="T")
                        pb.wait("vector", "T", cbz)
                        cz1 = pb.opw("vector", lambda v: v.tensor_copy(zbb[:], pzb[:]), inc="V")
                        cz2 = pb.op("scalar", lambda s: s.copy(nsb[:], pnum[0:64, :]), inc="S")
                        pb.wait("vector", "S", cz2)
                        sub, hh = divmod(h_loc, 2)
                        dta = hglob // 2
                        cfin = pb.opw("vector", lambda v: v.tensor_tensor(
                            out=osc[:], in0=nsb[:], in1=zbb[:], op=ALU.mult), inc="V")
                        pb.wait("tensor", "V", cfin)
                        pb.wait("scalar", "V", cfin)
                        pb.wait("sync", "V", cfin)
                        pb.dma("sync", APClass(attn_dram.tensor,
                               dta * 128 * TO + 64 * hh * TO + qc * 512,
                               [[TO, 64], [1, 512]]), osc[:], "ST")
                        pb.wait("vector", "ST")
            pb.phase_barrier()

        with contextlib.ExitStack() as phW:
            hW = phW.enter_context
            wbw = hW(nc.sbuf_tensor("wbw", [128, 2 * 2048], BF16))
            afb = hW(nc.sbuf_tensor("afb", [128, 2 * TO], BF16))
            pwo2 = hW(nc.psum_tensor("pwo2", [128, 1024], F32))
            for mt in range(16):
                lw = wcol3(wo, mt, wbw[:, (mt % 2) * 2048:(mt % 2 + 1) * 2048])
                pb.wait("tensor", "LD", lw)
                for kt in range(16):
                    pb.wait("sync", "T")
                    la = pb.dma("sync", afb[:, (kt % 2) * TO:(kt % 2 + 1) * TO],
                                attn_dram[kt], "LD2")
                    pb.wait("tensor", "LD2", la)
                    for ch in range(2):
                        cmm = pb.op("tensor", lambda t, kt=kt, ch=ch, mt=mt: t.matmul(
                            pwo2[:, ch * 512:(ch + 1) * 512],
                            wbw[:, (mt % 2) * 2048 + kt * 128:(mt % 2) * 2048 + (kt + 1) * 128],
                            afb[:, (kt % 2) * TO + ch * 512:(kt % 2) * TO + (ch + 1) * 512],
                            start=(kt == 0), stop=(kt == 15)),
                            inc="T" if (ch == 1) else None)
                for ch in range(2):
                    pb.wait("vector", "T")
                    cv = pb.op("vector", lambda v, mt=mt, ch=ch: v.tensor_copy(
                        acc2[:, mt * TO + ch * 512: mt * TO + (ch + 1) * 512],
                        pwo2[:, ch * 512:(ch + 1) * 512]), inc="V")
                    pb.wait("tensor", "V", cv)
                if dbg and mt == 0:
                    pb.wait("sync", "V")
                    pb.dma("sync", dbg_wo2[:, 0:TO], acc2[:, 0:TO], "ST")
                    pb.dma("sync", dbg_wo2[:, TO:TO + TO], afb[:, 0:TO], "ST")
                    pb.wait("sync", "ST")
                if dbg:
                    pb.wait("sync", "V")
                    pb.dma("sync", dbg_pr[:, 2 * mt:2 * mt + 2], acc2[:, 0:2], "ST")
                    pb.wait("sync", "ST")
                pb.wait("gpsimd", "T")
            pb.phase_barrier()


        with contextlib.ExitStack() as phP:
            hP = phP.enter_context
            zw = hP(nc.sbuf_tensor("zw", [128, D], BF16))
            w2b = hP(nc.sbuf_tensor("w2b", [128, D], BF16))
            ps1 = hP(nc.psum_tensor("ps1", [128, 512], F32))
            for at in range(64):
                lw = pb.dma("sync", wfl[:], w2t[at * 128:(at + 1) * 128, :], "LD")
                pb.wait("vector", "LD", lw)
                cw2 = pb.opw("vector", lambda v: v.tensor_copy(w2b[:], wfl[:]), inc="V")
                pb.wait("tensor", "V", cw2)
                for ch in range(4):
                    cmm = pb.op("tensor", lambda t, ch=ch: t.matmul(
                        ps1[:], h128[:], w2b[:, ch * 512:(ch + 1) * 512],
                        start=True, stop=True), inc="T")
                    pb.wait("vector", "T", cmm)
                    cv = pb.op("vector", lambda v, ch=ch: v.tensor_copy(
                        zw[:, ch * 512:(ch + 1) * 512], ps1[:]), inc="V")
                    pb.wait("tensor", "V", cv)
                pb.wait("sync", "V")
                pb.dma("sync", APClass(w2s_dram.tensor, at * D, [[128 * D, 64], [1, D]]),
                       zw[0:64, :], "ST")
                pb.dma("sync", APClass(w2s_dram.tensor, (at + 64) * D, [[128 * D, 64], [1, D]]),
                       zw[64:128, :], "ST")
                pb.wait("vector", "ST")
            for blk in range(64):
                lw = pb.dma("sync", w2b[:], w2s_dram[blk], "LD")
                pb.wait("tensor", "LD", lw)
                for ch in range(4):
                    cmm = pb.op("tensor", lambda t, ch=ch: t.matmul(
                        ps1[:], h64b[:], w2b[:, ch * 512:(ch + 1) * 512],
                        start=True, stop=True), inc="T")
                    pb.wait("vector", "T", cmm)
                    cv = pb.op("vector", lambda v, ch=ch: v.tensor_copy(
                        zw[:, ch * 512:(ch + 1) * 512], ps1[:]), inc="V")
                    pb.wait("tensor", "V", cv)
                pb.wait("sync", "V")
                for dmt in range(16):
                    pb.dma("sync", w2p_dram[dmt][blk], zw[:, dmt * 128:(dmt + 1) * 128], "ST")
                pb.wait("vector", "ST")
                pb.wait("tensor", "ST")
            pb.phase_barrier()

        with contextlib.ExitStack() as phF:
            hF = phF.enter_context
            hsg = hF(nc.sbuf_tensor("hsg", [128, 16 * TO], BF16))
            w13 = hF(nc.sbuf_tensor("w13", [128, 2 * 2048], BF16))
            w2sg = hF(nc.sbuf_tensor("w2sg", [128, 16 * 128], BF16))
            sil = hF(nc.sbuf_tensor("sil", [128, TO], BF16))
            oevb = hF(nc.sbuf_tensor("oevb", [128, 512], BF16))
            pg1 = hF(nc.psum_tensor("pg1", [128, 512], F32))
            pg3 = hF(nc.psum_tensor("pg3", [128, 512], F32))
            po0 = hF(nc.psum_tensor("po0", [128, 512], F32))
            if dbg:
                lwd = pb.dma("sync", w2sg[:, 0:128], w2p_dram[0][0], "LD")
                pb.wait("sync", "LD", lwd)
                pb.dma("sync", dbg_w2[:], w2sg[:, 0:128], "ST")
                pb.wait("sync", "ST")
            for sg in range(4):
                for bj in range(16):
                    blk = sg * 16 + bj
                    lw1 = wcol3(w1pt, blk, w13[:, 0:2048])
                    lw3 = wcol3(w3pt, blk, w13[:, 2048:4096])
                    pb.wait("tensor", "LD", lw3)
                    for ch in range(2):
                        for kt in range(16):
                            pb.op("tensor", lambda t, kt=kt, ch=ch: t.matmul(
                                pg1[:], w13[:, kt * 128:(kt + 1) * 128],
                                xn[:, kt * S + ch * 512: kt * S + (ch + 1) * 512],
                                start=(kt == 0), stop=(kt == 15)))
                        for kt in range(16):
                            cmm = pb.op("tensor", lambda t, kt=kt, ch=ch: t.matmul(
                                pg3[:], w13[:, 2048 + kt * 128: 2048 + (kt + 1) * 128],
                                xn[:, kt * S + ch * 512: kt * S + (ch + 1) * 512],
                                start=(kt == 0), stop=(kt == 15)), inc="T" if kt == 15 else None)
                        pb.wait("scalar", "T", cmm)
                        csl = pb.op("scalar", lambda s, ch=ch: s.activation(
                            out=sil[:, ch * 512:(ch + 1) * 512], in_=pg1[:], func=AF.Silu), inc="S")
                        pb.wait("vector", "S", csl)
                        chb = pb.opw("vector", lambda v, bj=bj, ch=ch: v.tensor_tensor(
                            out=hsg[:, bj * TO + ch * 512: bj * TO + (ch + 1) * 512],
                            in0=sil[:, ch * 512:(ch + 1) * 512], in1=pg3[:], op=ALU.mult), inc="V")
                        pb.wait("tensor", "V", chb)
                    pb.wait("gpsimd", "T")
                if dbg and sg == 0:
                    pb.wait("sync", "V")
                    pb.dma("sync", dbg_hb[:], hsg[:, 0:TO], "ST")
                    pb.wait("sync", "ST")
                for dmt in range(16):
                    lw2 = pb.dma("gpsimd", w2sg[:],
                                 APClass(w2p_dram.tensor, (dmt * 64 + sg * 16) * 16384,
                                         [[128, 128], [128 * 128, 16], [1, 128]]), "LD2")
                    pb.wait("tensor", "LD2", lw2)
                    for ch in range(2):
                        for bj in range(16):
                            cmm = pb.op("tensor", lambda t, bj=bj, ch=ch: t.matmul(
                                po0[:], w2sg[:, bj * 128:(bj + 1) * 128],
                                hsg[:, bj * TO + ch * 512: bj * TO + (ch + 1) * 512],
                                start=(bj == 0), stop=(bj == 15)), inc="T" if bj == 15 else None)
                        pb.wait("scalar", "T", cmm)
                        cev = pb.op("scalar", lambda s: s.copy(oevb[:], po0[:]), inc="S")
                        pb.wait("vector", "S", cev)
                        cv = pb.opw("vector", lambda v, dmt=dmt, ch=ch: v.tensor_tensor(
                            out=acc2[:, dmt * TO + ch * 512: dmt * TO + (ch + 1) * 512],
                            in0=acc2[:, dmt * TO + ch * 512: dmt * TO + (ch + 1) * 512],
                            in1=oevb[:], op=ALU.add), inc="V")
                        pb.wait("tensor", "S", cev)
                    pb.wait("gpsimd", "T")
            if dbg:
                pb.wait("sync", "V")
                pb.dma("sync", dbg_ff[:], acc2[:, 0:TO], "ST")
                pb.wait("sync", "ST")
            pb.phase_barrier()

        with contextlib.ExitStack() as phZ:
            hZ = phZ.enter_context
            x0t = hZ(nc.sbuf_tensor("x0t", [128, D], F32))
            x1t = hZ(nc.sbuf_tensor("x1t", [128, D], F32))
            xf32 = hZ(nc.sbuf_tensor("xf32", [128, 128], F32))
            ptf = hZ(nc.psum_tensor("ptf", [128, 512], BF16))
            for tt in range(8):
                ldx = pb.dma("sync", x0t[:], x0r[tt * 128:(tt + 1) * 128, :], "LD")
                pb.wait("vector", "LD", ldx)
                for dt in range(16):
                    j = dt % 4
                    ct = pb.op("tensor", lambda t, dt=dt, j=j, tt=tt: t.transpose(
                        ptf[:, j * 128:(j + 1) * 128],
                        acc2[:, dt * TO + tt * 128: dt * TO + (tt + 1) * 128], ident[:]), inc="T")
                    pb.wait("scalar", "T", ct)
                    cev = pb.op("scalar", lambda s, j=j: s.copy(xf32[:], ptf[:, j * 128:(j + 1) * 128]), inc="S")
                    pb.wait("vector", "S", cev)
                    pb.wait("tensor", "S", cev)
                    cv = pb.op("vector", lambda v, dt=dt: v.tensor_tensor(
                        out=x1t[:, dt * 128:(dt + 1) * 128],
                        in0=xf32[:],
                        in1=x0t[:, dt * 128:(dt + 1) * 128], op=ALU.add), inc="V")
                    pb.wait("scalar", "V", cv)
                pb.wait("sync", "V")
                pb.dma("sync", y[tt * 128:(tt + 1) * 128, :], x1t[:], "ST")
                pb.wait("vector", "ST")
        pb.wait("sync", "ST")
        pb.wait("gpsimd", "ST")
    return nc




def build_final(nin=11):
    nc = bass.Bass("TRN2", target_bir_lowering=False)
    TOq, D = 1024, 2048
    xs = [nc.dram_tensor(f"in{i}", [TOq, D], F32, kind="ExternalInput").ap()
          for i in range(nin)]
    y = nc.dram_tensor("y", [TOq, D], F32, kind="ExternalOutput").ap()
    nt = TOq // 128
    with program(nc) as (pb, hold):
        acc = hold(nc.sbuf_tensor("acc", [128, nt * D], F32))
        tmp = hold(nc.sbuf_tensor("tmp", [128, nt * D], F32))

        def tile(t, i):
            return t[:, i * D:(i + 1) * D]
        for i in range(nt):
            pb.dma("sync", tile(acc, i), xs[0][i * 128:(i + 1) * 128, :], "LD")
        for j in range(1, nin):
            for i in range(nt):
                pb.dma("sync", tile(tmp, i), xs[j][i * 128:(i + 1) * 128, :], "LD")
            pb.wait("vector", "LD")
            for i in range(nt):
                cv = pb.op("vector", lambda v, i=i: v.tensor_tensor(
                    out=tile(acc, i), in0=tile(acc, i), in1=tile(tmp, i),
                    op=ALU.add), inc="V")
            if j < nin - 1:
                pb.wait("sync", "V")
        pb.wait("sync", "V")
        for i in range(nt):
            pb.dma("sync", y[i * 128:(i + 1) * 128, :], tile(acc, i), "ST")
        pb.wait("sync", "ST")
    return nc

def _bucket(d):
    rp = np.maximum(d, 0)
    is_small = rp < 16
    with np.errstate(divide="ignore"):
        rl = 16 + (np.log(np.maximum(rp, 1).astype(np.float64) / 16) / math.log(8) * 16).astype(np.int64)
    return np.where(is_small, rp, np.minimum(rl, 31))


def _l0_common(Wq, Wk, Wv, Wo, W1, W2, W3, rel_bias):
    jj, qq = np.meshgrid(np.arange(64), np.arange(128), indexing="ij")
    hidx = (128 * (qq % 64) + 64 * (qq // 64) + jj).reshape(-1)
    H128 = (-1.0) ** np.array([bin(a & b).count("1") for a in range(128) for b in range(128)],
                              dtype=np.float64).reshape(128, 128)
    H64 = H128[:64, :64]
    h64in = np.zeros((128, 128), np.float32)
    h64in[:64, :64] = H64
    h64in[64:, 64:] = H64
    jj2, ii2 = np.meshgrid(np.arange(128), np.arange(128), indexing="ij")
    ind0 = np.zeros((32, 128 * 128), np.float32)
    ind1 = np.zeros((32, 128 * 128), np.float32)
    b0m = _bucket(ii2 - jj2)
    b1m = _bucket(ii2 - jj2 + 128)
    for c in range(32):
        ind0[c] = (b0m == c).reshape(-1)
        ind1[c] = (b1m == c).reshape(-1)
    qpat = np.zeros((16, 1024), np.float32)
    for q0 in range(8):
        qpat[q0, q0 * 128:(q0 + 1) * 128] = 1.0
    kpat = np.zeros((16, 2048), np.float32)
    for j in range(16):
        kpat[j, j * 128:(j + 1) * 128] = 1.0
    onepair = np.zeros((128, 2), np.float32)
    onepair[0:64, 0] = 1.0
    onepair[64:128, 1] = 1.0
    return {
        "wq": Wq, "wk": Wk, "wv": Wv, "wo": Wo,
        "w1pt": np.ascontiguousarray(W1[hidx].T),
        "w3pt": np.ascontiguousarray(W3[hidx].T),
        "w2t": np.ascontiguousarray(W2.T),
        "relb": rel_bias,
        "relb01": np.ascontiguousarray(rel_bias[[0, 31], :]),
        "ind0": ind0, "ind1": ind1,
        "qpat": qpat.astype(BF), "kpat16": kpat.astype(BF),
        "onepair": onepair, "identin3": np.eye(128, dtype=np.float32),
        "h128in": (H128 * 2 ** -6.5).astype(np.float32),
        "h64in": h64in,
    }


def _l0_percore(common, x0b, gamma):
    own = slice(gamma * 1024, gamma * 1024 + 1024)
    oth = slice((1 - gamma) * 1024, (1 - gamma) * 1024 + 1024)
    amap = np.zeros((2, 128), np.float32)
    for q0 in range(8):
        for k0 in range(16):
            i = q0 * 16 + k0
            if k0 == q0:
                continue
            if k0 == (q0 - 1) % 16:
                if q0 == 0 and gamma == 0:
                    amap[0, i] = 1.0
                continue
            if k0 < 8:
                amap[0 if k0 > q0 else 1, i] = 1.0
            else:
                amap[0 if gamma == 0 else 1, i] = 1.0
    d = dict(common)
    d["x0r"] = np.ascontiguousarray(np.concatenate([x0b[own], x0b[oth]], 0))
    d["amap"] = amap
    d["indw"] = common["ind1"] if gamma == 1 else np.zeros_like(common["ind1"])
    return d


def _mamba_percore(x1b, W_in, conv_w, conv_b, W_x, W_dt, b_dt, A_log, Dv, W_out, dh):
    half = np.arange(1024 * dh, 1024 * dh + 1024)
    other = np.arange(1024 * (1 - dh), 1024 * (1 - dh) + 1024)
    roll = np.concatenate([half, other])
    return {
        "xb": x1b,
        "w_in_x": np.ascontiguousarray(W_in[:, :2048]),
        "w_in_r": np.ascontiguousarray(W_in[:, 2048 + half]),
        **{f"cw{k}": np.ascontiguousarray(conv_w[roll, :, k].T) for k in range(4)},
        "cbT": np.ascontiguousarray(conv_b[roll][:, None]),
        "w_x": np.ascontiguousarray(W_x[roll, :]),
        "w_dt_h": np.ascontiguousarray(W_dt[:, half]),
        "b_dt_h": np.ascontiguousarray(b_dt[half].reshape(8, 128).T),
        "a_log_h": np.ascontiguousarray(A_log[half]),
        "d_h": np.ascontiguousarray(Dv[half].reshape(8, 128).T),
        "w_out_h": np.ascontiguousarray(W_out[half][:, roll]),
        "identin2": np.eye(128, dtype=np.float32),
    }, roll


def _ffn1_percore(x1full, gpk, gsc, upk, usc, dpk, dsc, c):
    idx = np.concatenate([np.arange(2048 * c, 2048 * (c + 1), 2),
                          np.arange(2048 * c + 1, 2048 * (c + 1), 2)])
    return {
        "x1": x1full,
        "gpk": np.ascontiguousarray(gpk[idx].T),
        "upk": np.ascontiguousarray(upk[idx].T),
        "dpk": np.ascontiguousarray(dpk[:, 1024 * c:1024 * (c + 1)].T),
        "gsc": np.ascontiguousarray(gsc[idx].T),
        "usc": np.ascontiguousarray(usc[idx].T),
        "dsc": np.ascontiguousarray(dsc[:, 64 * c:64 * (c + 1)].T),
        "identin": np.eye(128, dtype=np.float32),
    }


def kernel(**inputs):
    f32 = np.float32
    x0 = np.asarray(inputs["L0_attn_hidden_states"], f32)
    common = _l0_common(np.asarray(inputs["L0_attn_Wq"], f32),
                        np.asarray(inputs["L0_attn_Wk"], f32),
                        np.asarray(inputs["L0_attn_Wv"], f32),
                        np.asarray(inputs["L0_attn_Wo"], f32),
                        np.asarray(inputs["L0_ffn_W1"], f32),
                        np.asarray(inputs["L0_ffn_W2"], f32),
                        np.asarray(inputs["L0_ffn_W3"], f32),
                        np.asarray(inputs["L0_attn_rel_bias"], f32))
    nc0 = build_l0()
    maps = [_l0_percore(common, x0[c // 2], c % 2) for c in range(8)]
    res0 = run_bass_kernel_spmd(nc0, maps, core_ids=list(range(8)))
    x1 = np.empty((4, 2048, 2048), f32)
    for c in range(8):
        x1[c // 2, (c % 2) * 1024:(c % 2) * 1024 + 1024] = res0.results[c]["y"]

    nc2 = build_mamba()
    mmaps, rolls = [], []
    for c in range(8):
        m, roll = _mamba_percore(x1[c // 2],
                                 np.asarray(inputs["L1_attn_W_in"], f32),
                                 np.asarray(inputs["L1_attn_conv_w"], f32),
                                 np.asarray(inputs["L1_attn_conv_b"], f32),
                                 np.asarray(inputs["L1_attn_W_x"], f32),
                                 np.asarray(inputs["L1_attn_W_dt"], f32),
                                 np.asarray(inputs["L1_attn_b_dt"], f32),
                                 np.asarray(inputs["L1_attn_A_log"], f32),
                                 np.asarray(inputs["L1_attn_D"], f32),
                                 np.asarray(inputs["L1_attn_W_out"], f32),
                                 c % 2)
        mmaps.append(m)
        rolls.append(roll)
    res2 = run_bass_kernel_spmd(nc2, mmaps, core_ids=list(range(8)))
    mo = []
    for c in range(8):
        m = res2.results[c]["mo"]
        un = np.empty_like(m)
        un[rolls[c]] = m
        mo.append(np.ascontiguousarray(un.T))

    nc3 = build_ffn1()
    x1full = np.ascontiguousarray(x1.reshape(8192, 2048))
    fmaps = [_ffn1_percore(x1full,
                           np.asarray(inputs["L1_ffn_gate_W_packed"], np.int32),
                           np.asarray(inputs["L1_ffn_gate_scales"], f32),
                           np.asarray(inputs["L1_ffn_up_W_packed"], np.int32),
                           np.asarray(inputs["L1_ffn_up_scales"], f32),
                           np.asarray(inputs["L1_ffn_down_W_packed"], np.int32),
                           np.asarray(inputs["L1_ffn_down_scales"], f32), c)
             for c in range(8)]
    res3 = run_bass_kernel_spmd(nc3, fmaps, core_ids=list(range(8)))
    po = [np.ascontiguousarray(res3.results[c]["po"].T) for c in range(8)]

    nc4 = build_final(nin=11)
    gmaps = []
    for c in range(8):
        b, g = c // 2, c % 2
        rs = slice(b * 2048 + g * 1024, b * 2048 + g * 1024 + 1024)
        ls = slice(g * 1024, g * 1024 + 1024)
        ins = [np.ascontiguousarray(x1[b][ls]),
               np.ascontiguousarray(mo[2 * b][b * 0 + 0:0] if False else mo[2 * b][ls]),
               np.ascontiguousarray(mo[2 * b + 1][ls])]
        ins += [np.ascontiguousarray(po[j][rs]) for j in range(8)]
        gmaps.append({f"in{i}": ins[i] for i in range(11)})
    res4 = run_bass_kernel_spmd(nc4, gmaps, core_ids=list(range(8)))
    out = np.empty((4, 2048, 2048), f32)
    for c in range(8):
        out[c // 2, (c % 2) * 1024:(c % 2) * 1024 + 1024] = res4.results[c]["y"]
    return out



# version 03
# speedup vs baseline: 1.1703x; 1.1703x over previous
import contextlib
import math
import numpy as np
import ml_dtypes
import concourse.bass as bass
import concourse.mybir as mybir
from concourse.ap import AP as APClass
from concourse.bass_utils import run_bass_kernel_spmd

F32 = mybir.dt.float32
BF16 = mybir.dt.bfloat16
I32 = mybir.dt.int32
I16 = mybir.dt.int16
AF = mybir.ActivationFunctionType
ALU = mybir.AluOpType
AX = mybir.AxisListType
BF = ml_dtypes.bfloat16

NCORES = 8
ENGINES = ("sync", "scalar", "vector", "tensor", "gpsimd")

import hashlib
import os
import shutil
import concourse.bass_utils as _bu

if not getattr(_bu, "_neff_cache_installed", False):
    _orig_compile = _bu.compile_bir_kernel

    def _cached_compile(bir_json, tmpdir, neff_name="file.neff"):
        cdir = os.environ.get("BASS_NEFF_CACHE", "/tmp/bass_neff_cache")
        try:
            os.makedirs(cdir, exist_ok=True)
            key = hashlib.sha256(bir_json).hexdigest()
            cpath = os.path.join(cdir, key + ".neff")
            if os.path.exists(cpath):
                dst = os.path.join(tmpdir, neff_name)
                shutil.copyfile(cpath, dst)
                return dst
            out = _orig_compile(bir_json, tmpdir, neff_name)
            try:
                shutil.copyfile(out, cpath + ".tmp")
                os.replace(cpath + ".tmp", cpath)
            except OSError:
                pass
            return out
        except Exception:
            return _orig_compile(bir_json, tmpdir, neff_name)

    _bu.compile_bir_kernel = _cached_compile
    try:
        import concourse.bass2jax as _b2j
        _b2j.compile_bir_kernel = _cached_compile
    except ImportError:
        pass
    _bu._neff_cache_installed = True


class PB:

    SEMS = ("LD", "LD2", "ST", "V", "S", "T", "G", "X")

    def __init__(self, nc, block, semh):
        self.nc = nc
        self.block = block
        self.semh = semh
        self.cnt = {k: 0 for k in semh}

    def wait(self, eng, sem, val=None):
        v = self.cnt[sem] if val is None else val
        h = self.semh[sem]
        getattr(self.block, eng)(lambda e: e.wait_ge(h, v))
        return v

    def dma(self, eng, out, in_, sem="LD", **kw):
        self.cnt[sem] += 16
        h = self.semh[sem]
        def f(e, out=out, in_=in_, kw=kw):
            e.dma_start(out=out, in_=in_, **kw).then_inc(h, 16)
        getattr(self.block, eng)(f)
        return self.cnt[sem]

    def op(self, eng, fn, inc=None):
        if inc is None:
            getattr(self.block, eng)(fn)
            return None
        self.cnt[inc] += 1
        h = self.semh[inc]
        def f(e, fn=fn):
            fn(e).then_inc(h, 1)
        getattr(self.block, eng)(f)
        return self.cnt[inc]

    def opw(self, eng, fn, inc):
        c = self.op(eng, fn, inc=inc)
        self.wait(eng, inc, c)
        return c

    def phase_barrier(self, engines=ENGINES):
        for eng in engines:
            for sem in self.SEMS:
                if self.cnt[sem]:
                    self.wait(eng, sem)


@contextlib.contextmanager
def program(nc):
    with contextlib.ExitStack() as st:
        hold = st.enter_context
        semh = {k: hold(nc.semaphore("sem" + k)) for k in PB.SEMS}
        block = st.enter_context(nc.Block())
        yield PB(nc, block, semh), hold


def emit_dequant_tile(pb, w16, nib, sc1, sc2, plane_out, hi, sc_exp=None):
    V = "vector"
    n = nib[:]
    if hi:
        pb.op(V, lambda v: v.tensor_scalar(n, w16[:], 4, 15, ALU.logical_shift_right, ALU.bitwise_and))
    else:
        pb.op(V, lambda v: v.tensor_single_scalar(n, w16[:], 15, ALU.bitwise_and))
    a, b = sc1[:], sc2[:]
    pb.op(V, lambda v: v.tensor_single_scalar(a, n, 6, ALU.bitwise_and))
    pb.op(V, lambda v: v.tensor_single_scalar(a, a, 2, ALU.min))
    pb.op(V, lambda v: v.tensor_single_scalar(b, n, 1, ALU.bitwise_and))
    pb.op(V, lambda v: v.tensor_tensor(out=a, in0=a, in1=b, op=ALU.add))
    pb.op(V, lambda v: v.tensor_scalar(b, n, 6, 1, ALU.bitwise_and, ALU.logical_shift_right))
    pb.op(V, lambda v: v.tensor_scalar(b, b, 1, 125, ALU.max, ALU.add))
    pb.op(V, lambda v: v.tensor_single_scalar(b, b, 7, ALU.logical_shift_left))
    pb.op(V, lambda v: v.tensor_scalar(n, n, 8, 12, ALU.bitwise_and, ALU.logical_shift_left))
    pb.op(V, lambda v: v.tensor_tensor(out=b, in0=b, in1=n, op=ALU.bitwise_or))
    pb.op(V, lambda v: v.tensor_copy(plane_out[:], a))
    pb.opw(V, lambda v: v.tensor_tensor(out=plane_out[:], in0=plane_out[:],
                                        in1=b.bitcast(BF16), op=ALU.mult), inc="V")
    if sc_exp is not None:
        pb.opw(V, lambda v: v.tensor_tensor(out=plane_out[:], in0=plane_out[:],
                                            in1=sc_exp[:], op=ALU.mult), inc="V")


def emit_ln_tile(pb, xt, xn_out, stats, epsc, d_dim, out_dt_note="bf16"):
    s1, s2, mean, var, sv, rstd, sq = (stats[k] for k in
                                       ("s1", "s2", "mean", "var", "sv", "rstd", "sq"))
    iv = 1.0 / d_dim
    cv = pb.op("vector", lambda v: v.reduce_sum(s1[:], xt[:], axis=AX.X), inc="V")
    cs = pb.op("scalar", lambda s: s.activation(out=sq[:], in_=xt[:], func=AF.Square,
                                                accum_out=s2[:]), inc="S")
    pb.wait("scalar", "V", cv)
    cs2 = pb.op("scalar", lambda s: s.mul(mean[:], s1[:], iv), inc="S")
    pb.wait("vector", "S", cs2)
    pb.opw("vector", lambda v: v.tensor_tensor(out=var[:], in0=mean[:], in1=mean[:],
                                               op=ALU.mult), inc="V")
    cva = pb.opw("vector", lambda v: v.tensor_scalar(var[:], s2[:], iv, var[:],
                                                     ALU.mult, ALU.subtract), inc="V")
    pb.wait("scalar", "V", cva)
    css = pb.op("scalar", lambda s: s.activation(out=sv[:], in_=var[:], func=AF.Sqrt,
                                                 bias=epsc[:]), inc="S")
    pb.wait("vector", "S", css)
    pb.opw("vector", lambda v: v.reciprocal(rstd[:], sv[:]), inc="V")
    c = pb.op("vector", lambda v: v.tensor_scalar(xn_out[:], xt[:], mean[:], rstd[:],
                                                  ALU.subtract, ALU.mult), inc="V")
    return c


def build_ffn1(dbg=False):
    nc = bass.Bass("TRN2", target_bir_lowering=False)
    T_ALL, D, NS = 8192, 2048, 2048
    TC = 512
    NCH = T_ALL // TC
    x1 = nc.dram_tensor("x1", [T_ALL, D], F32, kind="ExternalInput").ap()
    gpk = nc.dram_tensor("gpk", [1024, NS], I32, kind="ExternalInput").ap()
    upk = nc.dram_tensor("upk", [1024, NS], I32, kind="ExternalInput").ap()
    dpk = nc.dram_tensor("dpk", [1024, D], I32, kind="ExternalInput").ap()
    gsc = nc.dram_tensor("gsc", [64, NS], F32, kind="ExternalInput").ap()
    usc = nc.dram_tensor("usc", [64, NS], F32, kind="ExternalInput").ap()
    dsc = nc.dram_tensor("dsc", [64, D], F32, kind="ExternalInput").ap()
    po = nc.dram_tensor("po", [D, T_ALL], F32, kind="ExternalOutput").ap()
    inter_d = nc.dram_tensor("inter_d", [NS // 128, NCH, 128, TC], BF16,
                             kind="ExternalOutput" if dbg else "Internal").ap()
    if dbg:
        dbg_g = nc.dram_tensor("dbg_g", [128, NS], F32, kind="ExternalOutput").ap()
        dbg_x = nc.dram_tensor("dbg_x", [128, 512], F32, kind="ExternalOutput").ap()
        dbg_d = nc.dram_tensor("dbg_d", [128, D], F32, kind="ExternalOutput").ap()
        dbg_ga = nc.dram_tensor("dbg_ga", [128, 512], F32, kind="ExternalOutput").ap()
        dbg_it = nc.dram_tensor("dbg_it", [128, 512], F32, kind="ExternalOutput").ap()

    with program(nc) as (pb, hold):
        stats = {k: hold(nc.sbuf_tensor("st_" + k, [128, 1], F32))
                 for k in ("s1", "s2", "mean", "var", "sv", "rstd")}
        stats["sq"] = None
        epsc = hold(nc.sbuf_tensor("epsc", [128, 1], F32))
        ident = hold(nc.sbuf_tensor("ident", [128, 128], BF16))
        pb.op("vector", lambda v: v.memset(epsc[:], 1e-5))
        identin = nc.dram_tensor("identin", [128, 128], F32, kind="ExternalInput").ap()
        pb.dma("gpsimd", ident[:], identin[:], "LD")
        pb.wait("tensor", "LD")

        def dequant_weight(scr, pk_ap, sc_ap, ncols, planes):
            w32, w16, nib, sc1, sc2, scex = scr
            """planes: list of 16 bf16 [128, ncols] tensors (tile t lo, tile t hi)."""
            for t in range(8):
                ld = pb.dma("sync", w32[:, 0:ncols], pk_ap[t * 128:(t + 1) * 128, :], "LD")
                for j in range(8):
                    r = 8 * t + j
                    pb.dma("gpsimd", scex[16 * j:16 * j + 16, 0:ncols],
                           sc_ap[r:r + 1, :].partition_broadcast(16)[:, 0, :], "LD2")
                pb.wait("vector", "LD")
                pb.wait("vector", "LD2")
                pb.opw("vector", lambda v, nc_=ncols: v.tensor_copy(w16[:, 0:nc_], w32[:, 0:nc_]), inc="V")
                emit_dequant_tile(pb, w16[:, 0:ncols], nib[:, 0:ncols], sc1[:, 0:ncols],
                                  sc2[:, 0:ncols], planes[2 * t][:, 0:ncols], False,
                                  scex[:, 0:ncols])
                emit_dequant_tile(pb, w16[:, 0:ncols], nib[:, 0:ncols], sc1[:, 0:ncols],
                                  sc2[:, 0:ncols], planes[2 * t + 1][:, 0:ncols], True,
                                  scex[:, 0:ncols])
                pb.wait("sync", "V")
                pb.wait("gpsimd", "V")

        with contextlib.ExitStack() as ph1:
            h1 = ph1.enter_context
            gpl = [h1(nc.sbuf_tensor(f"g{i}", [128, NS], BF16)) for i in range(16)]
            upl = [h1(nc.sbuf_tensor(f"u{i}", [128, NS], BF16)) for i in range(16)]
            with contextlib.ExitStack() as scs:
                hs = scs.enter_context
                scr = (hs(nc.sbuf_tensor("w32", [128, 2048], I32)),
                       hs(nc.sbuf_tensor("w16", [128, 2048], I16)),
                       hs(nc.sbuf_tensor("nib", [128, 2048], I16)),
                       hs(nc.sbuf_tensor("sc1", [128, 2048], I16)),
                       hs(nc.sbuf_tensor("sc2", [128, 2048], I16)),
                       hs(nc.sbuf_tensor("scex", [128, 2048], BF16)))
                dequant_weight(scr, gpk, gsc, NS, gpl)
                dequant_weight(scr, upk, usc, NS, upl)
            pb.phase_barrier()
            if dbg:
                pb.dma("gpsimd", dbg_g[:], gpl[0][:], "ST")

            xtile = h1(nc.sbuf_tensor("xtile", [128, D], F32))
            xnt = h1(nc.sbuf_tensor("xnt", [128, D], BF16))
            stats["sq"] = xnt
            xeo_b = h1(nc.sbuf_tensor("xeo_b", [128, 16 * TC], BF16))
            xeo = [xeo_b[:, i * TC:(i + 1) * TC] for i in range(16)]
            tpp = h1(nc.psum_tensor("tpp", [128, TC], BF16))
            pg = h1(nc.psum_tensor("pg", [128, 2 * TC], F32))
            pu = h1(nc.psum_tensor("pu", [128, 2 * TC], F32))
            gsq = h1(nc.sbuf_tensor("gsq", [128, TC], BF16))
            gp2 = h1(nc.sbuf_tensor("gp2", [128, TC], BF16))
            gt3 = h1(nc.sbuf_tensor("gt3", [128, TC], BF16))
            gt4 = h1(nc.sbuf_tensor("gt4", [128, TC], BF16))
            itl = h1(nc.sbuf_tensor("itl", [128, 2 * TC], BF16))

            for tc in range(NCH):
                for tt in range(TC // 128):
                    trow = tc * TC + tt * 128
                    ldx = pb.dma("sync", xtile[:], x1[trow:trow + 128, :], "LD")
                    pb.wait("vector", "LD", ldx)
                    pb.wait("scalar", "LD", ldx)
                    pb.wait("scalar", "T")
                    pb.wait("vector", "T")
                    cxn = emit_ln_tile(pb, xtile, xnt, stats, epsc, D)
                    pb.wait("tensor", "V", cxn)
                    for g4 in range(4):
                        for j in range(4):
                            idx = g4 * 4 + j
                            et, par = divmod(idx, 2)
                            src = xnt[:, et * 256 + par: et * 256 + par + 255: 2]
                            ct = pb.op("tensor", lambda t, src=src, j=j: t.transpose(
                                tpp[:, j * 128:(j + 1) * 128], src, ident[:]),
                                inc="T" if j == 3 else None)
                        pb.wait("vector", "T", ct)
                        dstap = APClass(xeo_b, (g4 * 4) * TC + tt * 128,
                                        [[16 * TC, 128], [TC, 4], [1, 128]])
                        cv = pb.op("vector", lambda v, dstap=dstap: v.tensor_copy(
                            dstap, tpp[:]), inc="V")
                        pb.wait("tensor", "V", cv)
                    pb.wait("sync", "V")
                pb.wait("tensor", "V")
                if dbg and tc == 0:
                    pb.wait("gpsimd", "V")
                    pb.dma("gpsimd", dbg_x[:], xeo[0][:], "ST")
                prev_fin = {}
                for nt in range(16):
                    nsl = slice(nt * 128, nt * 128 + 128)
                    pr = nt % 2
                    pgs = pg[:, pr * TC:(pr + 1) * TC]
                    pus = pu[:, pr * TC:(pr + 1) * TC]
                    its = itl[:, pr * TC:(pr + 1) * TC]
                    if nt >= 2:
                        pb.wait("tensor", "V", prev_fin[pr])
                    for ik in range(16):
                        wtile = (gpl[ik])[:, nsl]
                        ct = pb.op("tensor", lambda t, w=wtile, ik=ik, pgs=pgs: t.matmul(
                            pgs, w, xeo[ik][:], start=(ik == 0), stop=(ik == 15)),
                            inc="T" if ik == 15 else None)
                    for ik in range(16):
                        wtile = (upl[ik])[:, nsl]
                        ct2 = pb.op("tensor", lambda t, w=wtile, ik=ik, pus=pus: t.matmul(
                            pus, w, xeo[ik][:], start=(ik == 0), stop=(ik == 15)),
                            inc="T" if ik == 15 else None)
                    pb.wait("scalar", "T", ct2)
                    pb.wait("vector", "T", ct2)
                    cs = pb.op("scalar", lambda s, pgs=pgs: s.activation(out=gsq[:], in_=pgs,
                                                                func=AF.Square), inc="S")
                    pb.wait("vector", "S", cs)
                    pb.opw("vector", lambda v: v.tensor_scalar(
                        gp2[:], gsq[:], 0.0356774081, 0.7978845608, ALU.mult, ALU.add), inc="V")
                    cv = pb.opw("vector", lambda v, pgs=pgs: v.tensor_tensor(
                        out=gt3[:], in0=pgs, in1=gp2[:], op=ALU.mult), inc="V")
                    pb.wait("scalar", "V", cv)
                    cs2 = pb.op("scalar", lambda s: s.activation(out=gt4[:], in_=gt3[:],
                                                                 func=AF.Tanh), inc="S")
                    pb.wait("vector", "S", cs2)
                    pb.opw("vector", lambda v: v.tensor_scalar(
                        gt4[:], gt4[:], 1.0, 0.5, ALU.add, ALU.mult), inc="V")
                    pb.opw("vector", lambda v, pgs=pgs: v.tensor_tensor(
                        out=gt3[:], in0=pgs, in1=gt4[:], op=ALU.mult), inc="V")
                    cfin = pb.opw("vector", lambda v, pus=pus, its=its: v.tensor_tensor(
                        out=its, in0=pus, in1=gt3[:], op=ALU.mult), inc="V")
                    prev_fin[pr] = cfin
                    if dbg and tc == 5 and nt == 3:
                        cga = pb.op("vector", lambda v: v.tensor_copy(gsq[:], pg[:]), inc="V")
                        pb.wait("gpsimd", "V", cga)
                        pb.dma("gpsimd", dbg_ga[:], gsq[:], "ST")
                        pb.dma("gpsimd", dbg_it[:], itl[:], "ST")
                        pb.wait("vector", "ST")
                    pb.wait("sync", "V", cfin)
                    pb.dma("sync", inter_d[nt, tc], its, "ST")
                    if nt >= 1:
                        pb.wait("vector", "ST", pb.cnt["ST"] - 16)

        pb.phase_barrier()
        with contextlib.ExitStack() as ph2:
            h2 = ph2.enter_context
            dpl = [h2(nc.sbuf_tensor(f"d{i}", [128, D], BF16)) for i in range(16)]
            with contextlib.ExitStack() as scs:
                hs = scs.enter_context
                scr = (hs(nc.sbuf_tensor("w32b", [128, 2048], I32)),
                       hs(nc.sbuf_tensor("w16b", [128, 2048], I16)),
                       hs(nc.sbuf_tensor("nibb", [128, 2048], I16)),
                       hs(nc.sbuf_tensor("sc1b", [128, 2048], I16)),
                       hs(nc.sbuf_tensor("sc2b", [128, 2048], I16)),
                       hs(nc.sbuf_tensor("scexb", [128, 2048], BF16)))
                dequant_weight(scr, dpk, dsc, D, dpl)
            pb.phase_barrier()
            if dbg:
                pb.dma("gpsimd", dbg_d[:], dpl[0][:], "ST")
            itin = [h2(nc.sbuf_tensor(f"it{i}", [128, TC], BF16)) for i in range(16)]
            pd = h2(nc.psum_tensor("pd", [128, 2 * TC], F32))
            oevac = h2(nc.sbuf_tensor("oevac", [128, 2 * TC], F32))

            for tc in range(NCH):
                for nt in range(16):
                    pb.dma("sync", itin[nt][:], inter_d[nt, tc], "LD")
                pb.wait("tensor", "LD")
                prev_cv = {}
                for dmt in range(16):
                    dsl = slice(dmt * 128, dmt * 128 + 128)
                    pr = dmt % 2
                    pds = pd[:, pr * TC:(pr + 1) * TC]
                    oes = oevac[:, pr * TC:(pr + 1) * TC]
                    if dmt >= 2:
                        pb.wait("tensor", "V", prev_cv[pr])
                    for nt in range(16):
                        dw = dpl[2 * nt] if nt < 8 else dpl[2 * (nt - 8) + 1]
                        ct = pb.op("tensor", lambda t, dw=dw, nt=nt, dsl=dsl, pds=pds: t.matmul(
                            pds, dw[:, dsl], itin[nt][:],
                            start=(nt == 0), stop=(nt == 15)),
                            inc="T" if nt == 15 else None)
                    pb.wait("vector", "T", ct)
                    if dmt >= 2:
                        pb.wait("vector", "ST", pb.cnt["ST"] - 16)
                    cv = pb.op("vector", lambda v, pds=pds, oes=oes: v.tensor_copy(oes, pds), inc="V")
                    prev_cv[pr] = cv
                    pb.wait("sync", "V", cv)
                    pb.dma("sync", po[dmt * 128:dmt * 128 + 128, tc * TC:(tc + 1) * TC],
                           oes, "ST")
                pb.wait("sync", "T")
                pb.wait("vector", "ST")
        pb.wait("sync", "ST")
    return nc



def build_mamba(dbg=False):
    nc = bass.Bass("TRN2", target_bir_lowering=False)
    L, D, DH, NST, DTR = 2048, 2048, 1024, 16, 128
    LC = 512
    NLC = L // LC
    HDT = 8
    xb = nc.dram_tensor("xb", [L, D], F32, kind="ExternalInput").ap()
    w_in_x = nc.dram_tensor("w_in_x", [D, D], F32, kind="ExternalInput").ap()
    w_in_r = nc.dram_tensor("w_in_r", [D, DH], F32, kind="ExternalInput").ap()
    cw = [nc.dram_tensor(f"cw{k}", [D, D], F32, kind="ExternalInput").ap() for k in range(4)]
    cbT = nc.dram_tensor("cbT", [D, 1], F32, kind="ExternalInput").ap()
    w_x = nc.dram_tensor("w_x", [D, 160], F32, kind="ExternalInput").ap()
    w_dt_h = nc.dram_tensor("w_dt_h", [DTR, DH], F32, kind="ExternalInput").ap()
    b_dt_h = nc.dram_tensor("b_dt_h", [128, 8], F32, kind="ExternalInput").ap()
    a_log_h = nc.dram_tensor("a_log_h", [DH, NST], F32, kind="ExternalInput").ap()
    d_h = nc.dram_tensor("d_h", [128, 8], F32, kind="ExternalInput").ap()
    w_out_h = nc.dram_tensor("w_out_h", [DH, D], F32, kind="ExternalInput").ap()
    identin = nc.dram_tensor("identin2", [128, 128], F32, kind="ExternalInput").ap()
    mo = nc.dram_tensor("mo", [D, L], F32, kind="ExternalOutput").ap()
    bc_dram = nc.dram_tensor("bc_dram", [32, L], F32,
                             kind="ExternalOutput" if dbg else "Internal").ap()
    xs_dram = nc.dram_tensor("xs_dram", [16, 128, L], F32,
                             kind="ExternalOutput" if dbg else "Internal").ap()
    if dbg:
        dbg_xn = nc.dram_tensor("dbg_xn", [128, 2048], BF16, kind="ExternalOutput").ap()
        dbg_xc = nc.dram_tensor("dbg_xc", [128, 2048], BF16, kind="ExternalOutput").ap()
        dbg_dl = nc.dram_tensor("dbg_dl", [128, 512], F32, kind="ExternalOutput").ap()
        dbg_du = nc.dram_tensor("dbg_du", [128, 512], F32, kind="ExternalOutput").ap()
        dbg_y = nc.dram_tensor("dbg_y", [128, 512], F32, kind="ExternalOutput").ap()
        dbg_dbc = nc.dram_tensor("dbg_dbc", [128, 2048], F32, kind="ExternalOutput").ap()
        dbg_xna = nc.dram_tensor("dbg_xna", [128, 2048], BF16, kind="ExternalOutput").ap()
        dbg_xn1 = nc.dram_tensor("dbg_xn1", [128, 2048], BF16, kind="ExternalOutput").ap()
        dbg_tm = nc.dram_tensor("dbg_tm", [128, 512], F32, kind="ExternalOutput").ap()
        dbg_pd = nc.dram_tensor("dbg_pd", [128, 512], F32, kind="ExternalOutput").ap()
        dbg_dbc2 = nc.dram_tensor("dbg_dbc2", [128, 512], BF16, kind="ExternalOutput").ap()
        dbg_wdt = nc.dram_tensor("dbg_wdt", [128, 1024], BF16, kind="ExternalOutput").ap()

    with program(nc) as (pb, hold):
        stats = {k: hold(nc.sbuf_tensor("st_" + k, [128, 1], F32))
                 for k in ("s1", "s2", "mean", "var", "sv", "rstd")}
        epsc = hold(nc.sbuf_tensor("epsc", [128, 1], F32))
        onec = hold(nc.sbuf_tensor("onec", [128, 1], F32))
        ident = hold(nc.sbuf_tensor("ident", [128, 128], BF16))
        pb.op("vector", lambda v: v.memset(epsc[:], 1e-5))
        pb.op("vector", lambda v: v.memset(onec[:], 1.0))
        pb.dma("gpsimd", ident[:], identin[:], "LD")
        pb.wait("tensor", "LD")

        rT = hold(nc.sbuf_tensor("rT", [128, 8 * L], BF16))
        phABC = contextlib.ExitStack()
        xc = phABC.enter_context(nc.sbuf_tensor("xc", [128, 16 * (L + 8)], BF16))
        XCW = L + 8
        for i in range(16):
            pb.op("vector", lambda v, i=i: v.memset(xc[:, i * XCW:i * XCW + 8], 0.0))

        def load_wcol(wap, colt, dest, nk=16, sem="LD", width=128):
            K2 = wap.tensor.shape[1]
            src = APClass(wap.tensor, colt * width,
                          [[K2, 128], [128 * K2, nk], [1, width]])
            return pb.dma("gpsimd", dest, src, sem)

        with contextlib.ExitStack() as phA:
            hA = phA.enter_context
            xn = hA(nc.sbuf_tensor("xn", [128, 16 * L], BF16))
            xtile = hA(nc.sbuf_tensor("xtile", [128, D], F32))
            xnt = hA(nc.sbuf_tensor("xnt", [128, D], BF16))
            stats["sq"] = xnt
            tpp = hA(nc.psum_tensor("tpp", [128, 512], BF16))
            for tt in range(16):
                ldx = pb.dma("sync", xtile[:], xb[tt * 128:(tt + 1) * 128, :], "LD")
                pb.wait("vector", "LD", ldx)
                pb.wait("scalar", "LD", ldx)
                pb.wait("scalar", "T")
                pb.wait("vector", "T")
                cxn = emit_ln_tile(pb, xtile, xnt, stats, epsc, D)
                pb.wait("tensor", "V", cxn)
                for g4 in range(4):
                    for j in range(4):
                        dt = g4 * 4 + j
                        ct = pb.op("tensor", lambda t, dt=dt, j=j: t.transpose(
                            tpp[:, j * 128:(j + 1) * 128],
                            xnt[:, dt * 128:(dt + 1) * 128], ident[:]),
                            inc="T" if j == 3 else None)
                    pb.wait("vector", "T", ct)
                    dstap = APClass(xn, (g4 * 4) * L + tt * 128,
                                    [[16 * L, 128], [L, 4], [1, 128]])
                    cv = pb.op("vector", lambda v, dstap=dstap: v.tensor_copy(
                        dstap, tpp[:]), inc="V")
                    pb.wait("tensor", "V", cv)
                pb.wait("sync", "V")
            if dbg:
                pb.wait("gpsimd", "V")
                pb.wait("sync", "V")
                pb.dma("sync", dbg_xna[:], xn[:, 0:L], "ST")
                pb.wait("sync", "ST")
            wbuf = hA(nc.sbuf_tensor("wbuf", [128, 2 * D], BF16))
            pz = hA(nc.psum_tensor("pz", [128, 512], F32))
            for zt in range(24):
                wap = w_in_x if zt < 16 else w_in_r
                colt = zt if zt < 16 else zt - 16
                par = zt % 2
                lw = load_wcol(wap, colt, wbuf[:, par * D:(par + 1) * D])
                pb.wait("tensor", "LD", lw)
                for ch in range(4):
                    for kt in range(16):
                        cm = pb.op("tensor", lambda t, kt=kt, ch=ch, par=par: t.matmul(
                            pz[:], wbuf[:, par * D + kt * 128: par * D + (kt + 1) * 128],
                            xn[:, kt * L + ch * 512: kt * L + (ch + 1) * 512],
                            start=(kt == 0), stop=(kt == 15)), inc="T" if kt == 15 else None)
                    pb.wait("vector", "T", cm)
                    if zt < 16:
                        cv = pb.op("vector", lambda v, zt=zt, ch=ch: v.tensor_copy(
                            xc[:, zt * XCW + 8 + ch * 512: zt * XCW + 8 + (ch + 1) * 512],
                            pz[:]), inc="V")
                    else:
                        cv = pb.op("vector", lambda v, zt=zt, ch=ch: v.tensor_copy(
                            rT[:, (zt - 16) * L + ch * 512: (zt - 16) * L + (ch + 1) * 512],
                            pz[:]), inc="V")
                    pb.wait("tensor", "V", cv)
                pb.wait("gpsimd", "T")
            pb.phase_barrier()

        with contextlib.ExitStack() as phC:
            hC = phC.enter_context
            cwb = hC(nc.sbuf_tensor("cwb", [128, 8 * D], BF16))
            pcv = hC(nc.psum_tensor("pcv", [128, 512], F32))
            cbt = hC(nc.sbuf_tensor("cbt", [128, 1], F32))
            xse = hC(nc.sbuf_tensor("xse", [128, L], F32))
            for ot in range(16):
                par = ot % 2
                for k in range(4):
                    lw = load_wcol(cw[k], ot, cwb[:, (par * 4 + k) * D:(par * 4 + k + 1) * D])
                ldc = pb.dma("sync", cbt[:], cbT[ot * 128:(ot + 1) * 128, :], "LD")
                pb.wait("tensor", "LD", lw)
                pb.wait("tensor", "LD", ldc)
                for ch in range(4):
                    nmm = 0
                    for k in range(4):
                        for kt in range(16):
                            nmm += 1
                            cm = pb.op("tensor", lambda t, k=k, kt=kt, ch=ch, par=par, nmm=nmm: t.matmul(
                                pcv[:], cwb[:, (par * 4 + k) * D + kt * 128:(par * 4 + k) * D + (kt + 1) * 128],
                                xc[:, kt * XCW + 8 + ch * 512 - 3 + k: kt * XCW + 8 + ch * 512 - 3 + k + 512],
                                start=(nmm == 1), stop=(nmm == 64)),
                                inc="T" if nmm == 64 else None)
                    pb.wait("scalar", "T", cm)
                    pb.wait("scalar", "LD", ldc)
                    cs = pb.op("scalar", lambda s, ot=ot, ch=ch: s.add(
                        xse[:, ch * 512:(ch + 1) * 512], pcv[:], cbt[:]), inc="S")
                    pb.wait("tensor", "S", cs)
                pb.wait("sync", "S")
                pb.dma("sync", xs_dram[ot], xse[:], "ST")
                pb.wait("scalar", "ST")
                pb.wait("sync", "ST")
                pb.wait("gpsimd", "S")
            if dbg:
                pb.wait("gpsimd", "V")
                pb.wait("sync", "V")
                pb.dma("sync", dbg_xn[:], xn[:, 0:L], "ST")
                pb.dma("sync", dbg_xn1[:], xn[:, L:2*L], "ST")
                pb.dma("sync", dbg_xc[:], xc[:, 8:8+L], "ST")
                pb.wait("sync", "ST")
                pb.wait("gpsimd", "ST")
            pb.phase_barrier()
        phABC.close()

        dbcdt = hold(nc.sbuf_tensor("dbcdt", [128, L], F32))
        asb = hold(nc.sbuf_tensor("asb", [128, 8 * NST], F32))

        with contextlib.ExitStack() as phD:
            hD = phD.enter_context
            wxb = hD(nc.sbuf_tensor("wxb", [128, 16 * 160], BF16))
            src = APClass(w_x.tensor, 0, [[160, 128], [128 * 160, 16], [1, 160]])
            lw = pb.dma("gpsimd", wxb[:], src, "LD")
            usc = hD(nc.sbuf_tensor("usc", [128, 2 * 512], BF16))
            xsl = hD(nc.sbuf_tensor("xsl", [128, 2 * 512], F32))
            pdt = hD(nc.psum_tensor("pdt", [128, 512], F32))
            pbc = hD(nc.psum_tensor("pbc", [32, 512], F32))
            bcs = hD(nc.sbuf_tensor("bcs", [32, L], F32))
            pb.wait("tensor", "LD", lw)
            for ch in range(4):
                for kt in range(16):
                    par = kt % 2
                    lx = pb.dma("sync", xsl[:, par * 512:(par + 1) * 512],
                                xs_dram[kt][:, ch * 512:(ch + 1) * 512], "LD")
                    pb.wait("scalar", "LD", lx)
                    csi = pb.op("scalar", lambda s, par=par: s.activation(
                        out=usc[:, par * 512:(par + 1) * 512],
                        in_=xsl[:, par * 512:(par + 1) * 512], func=AF.Silu), inc="S")
                    pb.wait("tensor", "S", csi)
                    pb.op("tensor", lambda t, kt=kt, par=par: t.matmul(
                        pdt[:], wxb[:, kt * 160:kt * 160 + 128],
                        usc[:, par * 512:(par + 1) * 512],
                        start=(kt == 0), stop=(kt == 15)))
                    cm = pb.op("tensor", lambda t, kt=kt, par=par: t.matmul(
                        pbc[:], wxb[:, kt * 160 + 128:kt * 160 + 160],
                        usc[:, par * 512:(par + 1) * 512],
                        start=(kt == 0), stop=(kt == 15)), inc="T")
                    pb.wait("scalar", "T")
                    pb.wait("sync", "T")
                pb.wait("vector", "T", cm)
                cv1 = pb.op("vector", lambda v, ch=ch: v.tensor_copy(
                    dbcdt[:, ch * 512:(ch + 1) * 512], pdt[:]), inc="V")
                cv2 = pb.op("vector", lambda v, ch=ch: v.tensor_copy(
                    bcs[:, ch * 512:(ch + 1) * 512], pbc[:]), inc="V")
                pb.wait("tensor", "V", cv2)
            pb.wait("sync", "V")
            pb.dma("sync", bc_dram[:], bcs[:], "ST")
            aload = pb.dma("sync", asb[:],
                           APClass(a_log_h.tensor, 0, [[NST, 128], [128 * NST, 8], [1, NST]]),
                           "LD")
            pb.wait("scalar", "LD", aload)
            ca = pb.opw("scalar", lambda s: s.activation(out=asb[:], in_=asb[:], func=AF.Exp), inc="S")
            pb.wait("vector", "S", ca)
            can = pb.opw("vector", lambda v: v.tensor_single_scalar(asb[:], asb[:], -1.0, ALU.mult), inc="V")
            pb.wait("sync", "ST")
            if dbg:
                pb.wait("gpsimd", "V")
                pb.dma("gpsimd", dbg_dbc[:], dbcdt[:], "ST")
                pb.wait("gpsimd", "ST")
            pb.phase_barrier()

        with contextlib.ExitStack() as phE:
            hE = phE.enter_context
            wob = hE(nc.sbuf_tensor("wob", [128, 8 * D], BF16))
            wdtb = hE(nc.sbuf_tensor("wdtb", [128, DH], F32))
            bdtb = hE(nc.sbuf_tensor("bdtb", [128, 8], F32))
            dvb = hE(nc.sbuf_tensor("dvb", [128, 8], F32))
            lwo = pb.dma("gpsimd", wob[:],
                         APClass(w_out_h.tensor, 0, [[D, 128], [128 * D, 8], [1, D]]), "LD")
            lwdt = pb.dma("sync", wdtb[:], w_dt_h[:], "LD")
            pb.dma("sync", bdtb[:], b_dt_h[:], "LD")
            pb.dma("sync", dvb[:], d_h[:], "LD")
            delta = hE(nc.sbuf_tensor("delta", [128, 8 * LC], F32))
            du = hE(nc.sbuf_tensor("du", [128, 8 * LC], F32))
            yac = hE(nc.sbuf_tensor("yac", [128, 8 * LC], F32))
            omb = hE(nc.sbuf_tensor("omb", [128, 8 * LC], BF16))
            xsh = hE(nc.sbuf_tensor("xsh", [128, 8 * LC], F32))
            carry = hE(nc.sbuf_tensor("carry", [128, 128], F32))
            asc = hE(nc.sbuf_tensor("asc", [128, 2 * LC], F32))
            bsc = hE(nc.sbuf_tensor("bsc", [128, 2 * LC], F32))
            hsc = hE(nc.sbuf_tensor("hsc", [128, LC], F32))
            tmp = hE(nc.sbuf_tensor("tmp", [128, LC], F32))
            usc2 = hE(nc.sbuf_tensor("usc2", [128, LC], F32))
            oev = hE(nc.sbuf_tensor("oev", [128, LC], F32))
            bbc = hE(nc.sbuf_tensor("bbc", [128, 2 * LC], F32))
            cbc = hE(nc.sbuf_tensor("cbc", [128, 2 * LC], F32))
            pdl0 = hE(nc.psum_tensor("pdl", [128, LC], F32))
            pdl = pdl0 if not os.environ.get("MPROBE") else hE(nc.psum_tensor("pdlx", [128, LC], F32))
            pwo = hE(nc.psum_tensor("pwo", [128, LC], F32))
            pb.op("vector", lambda v: v.memset(carry[:], 0.0))
            pb.wait("tensor", "LD")
            pb.wait("scalar", "LD")
            for lc in range(NLC):
                for dt in range(HDT):
                    pb.dma("sync", xsh[:, dt * LC:(dt + 1) * LC],
                           xs_dram[dt][:, lc * LC:(lc + 1) * LC], "LD")
                pb.wait("scalar", "LD")
                pb.wait("vector", "LD")
                for dt in range(HDT):
                    dl = delta[:, dt * LC:(dt + 1) * LC]
                    duv = du[:, dt * LC:(dt + 1) * LC]
                    cm = pb.op("tensor", lambda t, dt=dt, lc=lc: t.matmul(
                        pdl[:], wdtb[:, dt * 128:(dt + 1) * 128],
                        dbcdt[:, lc * LC:(lc + 1) * LC], start=True, stop=True), inc="T")
                    if dbg and lc == 0 and dt == 0:
                        pb.wait("vector", "T", cm)
                        cdump = pb.op("vector", lambda v: v.tensor_copy(oev[:], pdl[:]), inc="V")
                        pb.wait("sync", "V", cdump)
                        pb.dma("sync", dbg_pd[:], oev[:], "ST")
                        pb.wait("scalar", "ST")
                        pb.wait("tensor", "V", cdump)
                    pb.wait("scalar", "T", cm)
                    cs = pb.opw("scalar", lambda s, dt=dt: s.add(
                        tmp[:], pdl[:], bdtb[:, dt:dt + 1]), inc="S")
                    if dbg and lc == 0 and dt == 0:
                        pb.wait("sync", "S", cs)
                        pb.dma("sync", dbg_tm[:], tmp[:], "ST")
                        pb.dma("sync", dbg_dbc2[:], dbcdt[:, 0:512], "ST")
                        pb.dma("sync", dbg_wdt[:], wdtb[:], "ST")
                        pb.wait("scalar", "ST")
                    pb.wait("tensor", "S", cs)
                    pb.wait("vector", "S", cs)
                    cv = pb.op("vector", lambda v, dl=dl: v.tensor_relu(dl, tmp[:]), inc="V")
                    cs = pb.opw("scalar", lambda s: s.activation(out=usc2[:], in_=tmp[:], func=AF.Abs), inc="S")
                    cs = pb.opw("scalar", lambda s: s.activation(out=usc2[:], in_=usc2[:], func=AF.Exp, scale=-1.0), inc="S")
                    cs = pb.opw("scalar", lambda s: s.activation(out=usc2[:], in_=usc2[:], func=AF.Ln, bias=onec[:]), inc="S")
                    pb.wait("vector", "S", cs)
                    pb.wait("vector", "V", cv)
                    cdl = pb.opw("vector", lambda v, dl=dl: v.tensor_tensor(
                        out=dl, in0=dl, in1=usc2[:], op=ALU.add), inc="V")
                    cu = pb.op("scalar", lambda s, dt=dt: s.activation(
                        out=usc2[:], in_=xsh[:, dt * LC:(dt + 1) * LC], func=AF.Silu), inc="S")
                    pb.wait("vector", "S", cu)
                    cdu = pb.opw("vector", lambda v, dl=dl, duv=duv: v.tensor_tensor(
                        out=duv, in0=dl, in1=usc2[:], op=ALU.mult), inc="V")
                    cz = pb.op("vector", lambda v, dt=dt: v.memset(yac[:, dt * LC:(dt + 1) * LC], 0.0), inc="V")
                    pb.wait("scalar", "V", cz)
                if dbg and lc == 0:
                    pb.wait("gpsimd", "V")
                    pb.dma("gpsimd", dbg_dl[:], delta[:, 0:LC], "ST")
                    pb.dma("gpsimd", dbg_du[:], du[:, 0:LC], "ST")
                    pb.wait("gpsimd", "ST")
                for n in range(NST):
                    par = n % 2
                    lb = pb.dma("gpsimd", bbc[:, par * LC:(par + 1) * LC],
                                APClass(bc_dram.tensor, n * L + lc * LC, [[0, 128], [1, LC]]),
                                "LD2")
                    lb2 = pb.dma("gpsimd", cbc[:, par * LC:(par + 1) * LC],
                                 APClass(bc_dram.tensor, (16 + n) * L + lc * LC, [[0, 128], [1, LC]]),
                                 "LD2")
                    pb.wait("vector", "LD2", lb2)
                    prev_chs = {}
                    for dt in range(HDT):
                        dp = dt % 2
                        ascs = asc[:, dp * LC:(dp + 1) * LC]
                        bscs = bsc[:, dp * LC:(dp + 1) * LC]
                        if dt >= 2:
                            pb.wait("scalar", "V", prev_chs[dp])
                        cs = pb.op("scalar", lambda s, dt=dt, n=n, ascs=ascs: s.activation(
                            out=ascs, in_=delta[:, dt * LC:(dt + 1) * LC], func=AF.Exp,
                            scale=asb[:, dt * NST + n: dt * NST + n + 1]), inc="S")
                        if dt >= 2:
                            pb.wait("vector", "V", prev_chs[dp])
                        cb = pb.op("vector", lambda v, dt=dt, par=par, bscs=bscs: v.tensor_tensor(
                            out=bscs, in0=du[:, dt * LC:(dt + 1) * LC],
                            in1=bbc[:, par * LC:(par + 1) * LC], op=ALU.mult), inc="V")
                        pb.wait("vector", "S", cs)
                        chs = pb.opw("vector", lambda v, n=n, dt=dt, lc=lc, ascs=ascs, bscs=bscs:
                            v.tensor_tensor_scan(
                                out=hsc[:], data0=ascs, data1=bscs,
                                initial=(0.0 if lc == 0 else carry[:, n * 8 + dt: n * 8 + dt + 1]),
                                op0=ALU.mult, op1=ALU.add), inc="V")
                        prev_chs[dp] = chs
                        cca = pb.op("vector", lambda v, n=n, dt=dt: v.tensor_copy(
                            carry[:, n * 8 + dt: n * 8 + dt + 1], hsc[:, LC - 1:LC]), inc="V")
                        ctm = pb.opw("vector", lambda v, par=par: v.tensor_tensor(
                            out=tmp[:], in0=hsc[:], in1=cbc[:, par * LC:(par + 1) * LC],
                            op=ALU.mult), inc="V")
                        cya = pb.opw("vector", lambda v, dt=dt: v.tensor_tensor(
                            out=yac[:, dt * LC:(dt + 1) * LC],
                            in0=yac[:, dt * LC:(dt + 1) * LC], in1=tmp[:], op=ALU.add), inc="V")
                    pb.wait("gpsimd", "V")
                    pb.wait("scalar", "V")
                if dbg and lc == 0:
                    pb.wait("gpsimd", "V")
                    pb.dma("gpsimd", dbg_y[:], yac[:, 0:LC], "ST")
                    pb.wait("gpsimd", "ST")
                for dt in range(HDT):
                    cu = pb.op("scalar", lambda s, dt=dt: s.activation(
                        out=usc2[:], in_=xsh[:, dt * LC:(dt + 1) * LC], func=AF.Silu), inc="S")
                    pb.wait("vector", "S", cu)
                    pb.opw("vector", lambda v, dt=dt: v.tensor_scalar(
                        tmp[:], usc2[:], dvb[:, dt:dt + 1], None, ALU.mult), inc="V")
                    pb.opw("vector", lambda v, dt=dt: v.tensor_tensor(
                        out=yac[:, dt * LC:(dt + 1) * LC],
                        in0=yac[:, dt * LC:(dt + 1) * LC], in1=tmp[:], op=ALU.add), inc="V")
                    cr = pb.op("scalar", lambda s, dt=dt, lc=lc: s.activation(
                        out=usc2[:], in_=rT[:, dt * L + lc * LC: dt * L + (lc + 1) * LC],
                        func=AF.Silu), inc="S")
                    pb.wait("vector", "S", cr)
                    com = pb.opw("vector", lambda v, dt=dt: v.tensor_tensor(
                        out=omb[:, dt * LC:(dt + 1) * LC],
                        in0=yac[:, dt * LC:(dt + 1) * LC], in1=usc2[:], op=ALU.mult), inc="V")
                    pb.wait("scalar", "V", com)
                pb.wait("tensor", "V")
                for mt in range(16):
                    for dt in range(HDT):
                        cm = pb.op("tensor", lambda t, mt=mt, dt=dt: t.matmul(
                            pwo[:], wob[:, dt * D + mt * 128: dt * D + (mt + 1) * 128],
                            omb[:, dt * LC:(dt + 1) * LC],
                            start=(dt == 0), stop=(dt == HDT - 1)),
                            inc="T" if dt == HDT - 1 else None)
                    pb.wait("vector", "T", cm)
                    if mt < 8:
                        cv = pb.op("vector", lambda v, mt=mt: v.tensor_tensor(
                            out=oev[:], in0=pwo[:], in1=xsh[:, mt * LC:(mt + 1) * LC],
                            op=ALU.add), inc="V")
                    else:
                        cv = pb.op("vector", lambda v: v.tensor_copy(oev[:], pwo[:]), inc="V")
                    pb.wait("tensor", "V", cv)
                    pb.wait("sync", "V", cv)
                    pb.dma("sync", mo[mt * 128:(mt + 1) * 128, lc * LC:(lc + 1) * LC],
                           oev[:], "ST")
                    pb.wait("vector", "ST")
                pb.wait("scalar", "V")
                pb.wait("sync", "V")
        pb.wait("sync", "ST")
        pb.wait("gpsimd", "ST")
    return nc



def build_l0(dbg=False):
    nc = bass.Bass("TRN2", target_bir_lowering=False)
    S, D, NH, DKV, HM = 2048, 2048, 32, 64, 8192
    TO = 1024
    x0r = nc.dram_tensor("x0r", [S, D], F32, kind="ExternalInput").ap()
    wq = nc.dram_tensor("wq", [D, D], F32, kind="ExternalInput").ap()
    wk = nc.dram_tensor("wk", [D, D], F32, kind="ExternalInput").ap()
    wv = nc.dram_tensor("wv", [D, D], F32, kind="ExternalInput").ap()
    wo = nc.dram_tensor("wo", [D, D], F32, kind="ExternalInput").ap()
    w1pt = nc.dram_tensor("w1pt", [D, HM], F32, kind="ExternalInput").ap()
    w3pt = nc.dram_tensor("w3pt", [D, HM], F32, kind="ExternalInput").ap()
    w2t = nc.dram_tensor("w2t", [HM, D], F32, kind="ExternalInput").ap()
    relb = nc.dram_tensor("relb", [32, 32], F32, kind="ExternalInput").ap()
    relb01 = nc.dram_tensor("relb01", [2, 32], F32, kind="ExternalInput").ap()
    amap = nc.dram_tensor("amap", [2, 128], F32, kind="ExternalInput").ap()
    ind0 = nc.dram_tensor("ind0", [32, 16384], F32, kind="ExternalInput").ap()
    ind1 = nc.dram_tensor("ind1", [32, 16384], F32, kind="ExternalInput").ap()
    indw = nc.dram_tensor("indw", [32, 16384], F32, kind="ExternalInput").ap()
    qpat = nc.dram_tensor("qpat", [16, 1024], BF16, kind="ExternalInput").ap()
    kpat16 = nc.dram_tensor("kpat16", [16, 2048], BF16, kind="ExternalInput").ap()
    onepair = nc.dram_tensor("onepair", [128, 2], F32, kind="ExternalInput").ap()
    identin = nc.dram_tensor("identin3", [128, 128], F32, kind="ExternalInput").ap()
    h128in = nc.dram_tensor("h128in", [128, 128], F32, kind="ExternalInput").ap()
    h64in = nc.dram_tensor("h64in", [128, 128], F32, kind="ExternalInput").ap()
    y = nc.dram_tensor("y", [TO, D], F32, kind="ExternalOutput").ap()

    bm_dram = nc.dram_tensor("bm_dram", [32, 128], F32, kind="Internal").ap()
    dt_dram = nc.dram_tensor("dt_dram", [3, 32, 16384], BF16, kind="Internal").ap()
    w2s_dram = nc.dram_tensor("w2s_dram", [64, 128, D], BF16, kind="Internal").ap()
    attn_dram = nc.dram_tensor("attn_dram", [16, 128, TO], BF16,
                               kind="ExternalOutput" if dbg else "Internal").ap()
    if dbg:
        dbg_xn = nc.dram_tensor("dbg_xn", [128, 2048], BF16, kind="ExternalOutput").ap()
        dbg_wo = nc.dram_tensor("dbg_wo", [128, 1024], BF16, kind="ExternalOutput").ap()
        dbg_ff = nc.dram_tensor("dbg_ff", [128, 1024], BF16, kind="ExternalOutput").ap()
        dbg_hb = nc.dram_tensor("dbg_hb", [128, 1024], BF16, kind="ExternalOutput").ap()
        dbg_w2 = nc.dram_tensor("dbg_w2", [128, 128], BF16, kind="ExternalOutput").ap()
        dbg_ka = nc.dram_tensor("dbg_ka", [81, 2048], BF16, kind="ExternalOutput").ap()
        dbg_qa = nc.dram_tensor("dbg_qa", [81, 1024], BF16, kind="ExternalOutput").ap()
        dbg_wo2 = nc.dram_tensor("dbg_wo2", [128, 2048], BF16, kind="ExternalOutput").ap()
        dbg_pr = nc.dram_tensor("dbg_pr", [128, 32], BF16, kind="ExternalOutput").ap()
    w2p_dram = nc.dram_tensor("w2p_dram", [16, 64, 128, 128], BF16, kind="Internal").ap()

    with program(nc) as (pb, hold):
        stats = {k: hold(nc.sbuf_tensor("st_" + k, [128, 1], F32))
                 for k in ("s1", "s2", "mean", "var", "sv", "rstd")}
        epsc = hold(nc.sbuf_tensor("epsc", [128, 1], F32))
        ident = hold(nc.sbuf_tensor("ident", [128, 128], BF16))
        identf = hold(nc.sbuf_tensor("identf", [128, 128], F32))
        h128 = hold(nc.sbuf_tensor("h128", [128, 128], BF16))
        h64b = hold(nc.sbuf_tensor("h64b", [128, 128], BF16))
        onep = hold(nc.sbuf_tensor("onep", [128, 2], BF16))
        ones64 = hold(nc.sbuf_tensor("ones64", [1, 64], BF16))
        wfl = hold(nc.sbuf_tensor("wfl", [128, 2048], F32))
        pb.op("vector", lambda v: v.memset(epsc[:], 1e-5))
        pb.op("vector", lambda v: v.memset(ones64[:], 1.0))
        l1 = pb.dma("sync", identf[:], identin[:], "LD")
        pb.wait("vector", "LD", l1)
        pb.opw("vector", lambda v: v.tensor_copy(ident[:], identf[:]), inc="V")
        l2 = pb.dma("sync", wfl[:, 0:128], h128in[:], "LD")
        l3 = pb.dma("sync", wfl[:, 128:256], h64in[:], "LD")
        l4 = pb.dma("sync", wfl[:, 256:258], onepair[:], "LD")
        pb.wait("vector", "LD", l4)
        pb.opw("vector", lambda v: v.tensor_copy(h128[:], wfl[:, 0:128]), inc="V")
        pb.opw("vector", lambda v: v.tensor_copy(h64b[:], wfl[:, 128:256]), inc="V")
        pb.opw("vector", lambda v: v.tensor_copy(onep[:], wfl[:, 256:258]), inc="V")
        pb.wait("tensor", "V")
        pb.wait("sync", "V")

        def wcol3(wap, colt, dest, nk=16, width=128, sem="LD"):
            K2 = wap.tensor.shape[1]
            src = APClass(wap.tensor, colt * width,
                          [[K2, 128], [128 * K2, nk], [1, width]])
            return pb.dma("gpsimd", dest, src, sem)

        with contextlib.ExitStack() as phB:
            hB = phB.enter_context
            rb = hB(nc.sbuf_tensor("rb", [32, 32], F32))
            rb01 = hB(nc.sbuf_tensor("rb01", [2, 32], F32))
            am = hB(nc.sbuf_tensor("am", [2, 128], F32))
            indc = hB(nc.sbuf_tensor("indc", [32, 2048], F32))
            dsc = hB(nc.sbuf_tensor("dsc", [32, 2048], BF16))
            pbm = hB(nc.psum_tensor("pbm", [32, 512], F32))
            la = pb.dma("sync", rb[:], relb[:], "LD")
            lb = pb.dma("sync", rb01[:], relb01[:], "LD")
            lcm = pb.dma("sync", am[:], amap[:], "LD")
            pb.wait("tensor", "LD", lcm)
            cm = pb.op("tensor", lambda t: t.matmul(pbm[:, 0:128], rb01[:], am[:],
                                                    start=True, stop=True), inc="T")
            pb.wait("vector", "T", cm)
            cv = pb.op("vector", lambda v: v.tensor_copy(indc[:, 0:128], pbm[:, 0:128]), inc="V")
            pb.wait("sync", "V", cv)
            pb.dma("sync", bm_dram[:], indc[:, 0:128], "ST")
            pb.wait("tensor", "ST")
            for bi, iap in enumerate((ind0, ind1, indw)):
                for ch in range(8):
                    lic = pb.dma("sync", indc[:], iap[:, ch * 2048:(ch + 1) * 2048], "LD")
                    pb.wait("tensor", "LD", lic)
                    for q4 in range(4):
                        cm = pb.op("tensor", lambda t, q4=q4: t.matmul(
                            pbm[:], rb[:], indc[:, q4 * 512:(q4 + 1) * 512],
                            start=True, stop=True), inc="T")
                        pb.wait("vector", "T", cm)
                        cv = pb.op("vector", lambda v, q4=q4: v.tensor_copy(
                            dsc[:, q4 * 512:(q4 + 1) * 512], pbm[:]), inc="V")
                        pb.wait("tensor", "V", cv)
                    pb.wait("sync", "V")
                    pb.dma("sync", dt_dram[bi][:, ch * 2048:(ch + 1) * 2048], dsc[:], "ST")
                    pb.wait("sync", "ST")
                pb.wait("tensor", "ST")

        xn = hold(nc.sbuf_tensor("xn", [128, 16 * S], BF16))

        with contextlib.ExitStack() as phL:
            hL = phL.enter_context
            xtile = hL(nc.sbuf_tensor("xtile", [128, D], F32))
            xnt = hL(nc.sbuf_tensor("xnt", [128, D], BF16))
            stats["sq"] = xnt
            tpp = hL(nc.psum_tensor("tpp", [128, 512], BF16))
            for tt in range(16):
                ldx = pb.dma("sync", xtile[:], x0r[tt * 128:(tt + 1) * 128, :], "LD")
                pb.wait("vector", "LD", ldx)
                pb.wait("scalar", "LD", ldx)
                pb.wait("scalar", "T")
                pb.wait("vector", "T")
                cxn = emit_ln_tile(pb, xtile, xnt, stats, epsc, D)
                pb.wait("tensor", "V", cxn)
                for g4 in range(4):
                    for j in range(4):
                        dt = g4 * 4 + j
                        ct = pb.op("tensor", lambda t, dt=dt, j=j: t.transpose(
                            tpp[:, j * 128:(j + 1) * 128],
                            xnt[:, dt * 128:(dt + 1) * 128], ident[:]),
                            inc="T" if j == 3 else None)
                    pb.wait("vector", "T", ct)
                    dstap = APClass(xn, (g4 * 4) * S + tt * 128,
                                    [[16 * S, 128], [S, 4], [1, 128]])
                    cv = pb.op("vector", lambda v, dstap=dstap: v.tensor_copy(
                        dstap, tpp[:]), inc="V")
                    pb.wait("tensor", "V", cv)
                pb.wait("sync", "V")
            if dbg:
                pb.wait("sync", "V")
                pb.dma("sync", dbg_xn[:], xn[:, 0:S], "ST")
                pb.wait("sync", "ST")
            pb.phase_barrier()

        acc2 = hold(nc.sbuf_tensor("acc2", [128, 16 * TO], BF16))
        with contextlib.ExitStack() as phT:
            hT = phT.enter_context
            kT = hT(nc.sbuf_tensor("kT", [128, 2 * S], BF16))
            vT = hT(nc.sbuf_tensor("vT", [128, 2 * S], BF16))
            qT = hT(nc.sbuf_tensor("qT", [128, 2 * TO], BF16))
            kaug = [hT(nc.sbuf_tensor(f"kaug{i}", [81, S], BF16)) for i in range(4)]
            qaug = [hT(nc.sbuf_tensor(f"qaug{i}", [81, TO], BF16)) for i in range(4)]
            vaug = [hT(nc.sbuf_tensor(f"vaug{i}", [128, 16 * 65], BF16)) for i in range(4)]
            d0t = [hT(nc.sbuf_tensor(f"d0t{i}", [128, 3 * 128], BF16)) for i in range(4)]
            bmt = hT(nc.sbuf_tensor("bmt", [16, TO], BF16))
            bm3 = hT(nc.sbuf_tensor("bm3", [16, 512], F32))
            qp = hT(nc.sbuf_tensor("qp", [16, TO], BF16))
            bm3b = hT(nc.sbuf_tensor("bm3b", [16, 512], BF16))
            qqs = hT(nc.sbuf_tensor("qqs", [2, TO], F32))
            kks = hT(nc.sbuf_tensor("kks", [2, 512], F32))
            kkc = hT(nc.sbuf_tensor("kkc", [2, 1], F32))
            kkm = hT(nc.sbuf_tensor("kkm", [2, 1], F32))
            mrowb = hT(nc.sbuf_tensor("mrowb", [2, TO], BF16))
            atile = hT(nc.sbuf_tensor("atile", [128, 4 * 512], BF16))
            sqs = atile
            zr = hT(nc.sbuf_tensor("zr", [1, 512], F32))
            zrb = hT(nc.sbuf_tensor("zrb", [1, 512], BF16))
            zbb = hT(nc.sbuf_tensor("zbb", [64, 512], BF16))
            nsb = hT(nc.sbuf_tensor("nsb", [64, 512], BF16))
            osc = hT(nc.sbuf_tensor("osc", [64, 512], BF16))
            wbg = hT(nc.sbuf_tensor("wbg", [128, 2048], BF16))
            sp = hT(nc.psum_tensor("sp", [128, 4 * 512], F32))
            pnum = hT(nc.psum_tensor("pnum", [65, 512], F32))
            pzb = hT(nc.psum_tensor("pzb", [64, 512], F32))
            pqq = hT(nc.psum_tensor("pqq", [2, 512], F32))
            ptr = hT(nc.psum_tensor("ptr", [128, 512], BF16))

            pb.op("vector", lambda v: v.memset(bm3[:], 0.0))
            pb.wait("sync", "V")
            lq = pb.dma("sync", bm3[0:8, :], APClass(bm_dram.tensor, 0,
                        [[16, 8], [128, 32], [1, 16]]), "LD")
            pb.wait("vector", "LD", lq)
            pb.opw("vector", lambda v: v.tensor_copy(bm3b[:], bm3[:]), inc="V")
            pb.dma("sync", qp[:], qpat[:], "LD")
            for g in range(4):
                pb.op("vector", lambda v, g=g: v.memset(kaug[g][64:65, :], 1.0))
                lk = pb.dma("sync", kaug[g][65:81, :], kpat16[:], "LD")
                pb.op("vector", lambda v, g=g: v.memset(
                    APClass(vaug[g], 64, [[16 * 65, 128], [65, 16]]), 1.0))
            pb.wait("tensor", "LD")

            for grp in range(8):
                for which, wap, dest, ntok in (("q", wq, qT, TO), ("k", wk, kT, S), ("v", wv, vT, S)):
                    for sub in range(2):
                        colt = grp * 2 + sub
                        pb.wait("gpsimd", "T")
                        lw = wcol3(wap, colt, wbg[:])
                        pb.wait("tensor", "LD", lw)
                        nch = ntok // 512
                        for ch in range(nch):
                            for kt in range(16):
                                cmm = pb.op("tensor", lambda t, kt=kt, ch=ch: t.matmul(
                                    sp[:, 0:512], wbg[:, kt * 128:(kt + 1) * 128],
                                    xn[:, kt * S + ch * 512: kt * S + (ch + 1) * 512],
                                    start=(kt == 0), stop=(kt == 15)), inc="T" if kt == 15 else None)
                            pb.wait("vector", "T", cmm)
                            cv = pb.op("vector", lambda v, dest=dest, sub=sub, ch=ch: v.tensor_copy(
                                dest[:, sub * (dest.shape[1] // 2) + ch * 512:
                                     sub * (dest.shape[1] // 2) + (ch + 1) * 512], sp[:, 0:512]), inc="V")
                            pb.wait("tensor", "V", cv)
                pb.wait("scalar", "V")
                for sub in range(2):
                    cq = pb.op("scalar", lambda s, sub=sub: s.activation(
                        out=sqs[:, 0:TO], in_=qT[:, sub * TO:(sub + 1) * TO], func=AF.Square), inc="S")
                    pb.wait("tensor", "S", cq)
                    for ch in range(2):
                        cm = pb.op("tensor", lambda t, ch=ch: t.matmul(
                            pqq[:], onep[:], sqs[:, ch * 512:(ch + 1) * 512],
                            start=True, stop=True), inc="T")
                        pb.wait("vector", "T", cm)
                        cv = pb.opw("vector", lambda v, ch=ch: v.tensor_copy(
                            qqs[:, ch * 512:(ch + 1) * 512], pqq[:]), inc="V")
                        pb.wait("tensor", "V", cv)
                    pb.wait("scalar", "V")
                    ck = pb.op("scalar", lambda s, sub=sub: s.activation(
                        out=sqs[:], in_=kT[:, sub * S:(sub + 1) * S], func=AF.Square), inc="S")
                    pb.wait("tensor", "S", ck)
                    for ch in range(4):
                        cm = pb.op("tensor", lambda t, ch=ch: t.matmul(
                            pqq[:], onep[:], sqs[:, ch * 512:(ch + 1) * 512],
                            start=True, stop=True), inc="T")
                        pb.wait("vector", "T", cm)
                        cv = pb.opw("vector", lambda v: v.tensor_copy(kks[:], pqq[:]), inc="V")
                        pb.wait("tensor", "V", cv)
                        cv = pb.opw("vector", lambda v, ch=ch: v.reduce_max(
                            kkc[:] if ch else kkm[:], kks[:], axis=AX.X), inc="V")
                        if ch:
                            cv = pb.opw("vector", lambda v: v.tensor_tensor(
                                out=kkm[:], in0=kkm[:], in1=kkc[:], op=ALU.max), inc="V")
                    pb.wait("scalar", "V", cv)
                    csq = pb.op("scalar", lambda s: s.activation(out=kkm[:], in_=kkm[:], func=AF.Sqrt), inc="S")
                    pb.wait("scalar", "S", csq)
                    cs2 = pb.op("scalar", lambda s: s.activation(out=qqs[:], in_=qqs[:], func=AF.Sqrt), inc="S")
                    pb.wait("vector", "S", cs2)
                    cmr = pb.opw("vector", lambda v: v.tensor_scalar(
                        qqs[:], qqs[:], kkm[:], -40.0, ALU.mult, ALU.add), inc="V")
                    cmb = pb.opw("vector", lambda v: v.tensor_single_scalar(
                        mrowb[:], qqs[:], -1.0, ALU.mult), inc="V")
                    pb.wait("sync", "V", cmb)
                    for hh in range(2):
                        h_loc = sub * 2 + hh
                        pb.dma("sync", qaug[h_loc][64:65, :], mrowb[hh:hh + 1, :], "X")
                pb.wait("sync", "V")
                for h_loc in range(4):
                    sub, hh = divmod(h_loc, 2)
                    pb.dma("sync", kaug[h_loc][0:64, :], kT[:, sub * S:(sub + 1) * S].bitcast(BF16) if False else
                           APClass(kT, sub * S * 1 + hh * 64 * (2 * S), [[2 * S, 64], [1, S]]), "X")
                    pb.dma("sync", qaug[h_loc][0:64, :],
                           APClass(qT, sub * TO + hh * 64 * (2 * TO), [[2 * TO, 64], [1, TO]]), "X")
                    hglob = grp * 4 + h_loc
                    for ch in range(2):
                        cmb2 = pb.op("tensor", lambda t, hglob=hglob, ch=ch: t.matmul(
                            pzb[0:16, :], bm3b[:, hglob * 16:(hglob + 1) * 16],
                            qp[:, ch * 512:(ch + 1) * 512],
                            start=True, stop=True), inc="T")
                        pb.wait("vector", "T", cmb2)
                        cvb = pb.op("vector", lambda v, ch=ch: v.tensor_copy(
                            bmt[0:16, ch * 512:(ch + 1) * 512], pzb[0:16, :]), inc="V")
                        pb.wait("tensor", "V", cvb)
                    pb.wait("sync", "V")
                    pb.dma("sync", qaug[h_loc][65:81, :], bmt[0:16, :], "X")
                    pb.wait("vector", "X")
                    for bi in range(3):
                        pb.dma("sync", d0t[h_loc][:, bi * 128:(bi + 1) * 128],
                               APClass(dt_dram.tensor, bi * 32 * 16384 + hglob * 16384,
                                       [[128, 128], [1, 128]]), "X")
                    for k0 in range(16):
                        ct = pb.op("tensor", lambda t, sub=sub, hh=hh, k0=k0: t.transpose(
                            ptr[:, 0:64],
                            APClass(vT, sub * S + hh * 64 * (2 * S) + k0 * 128,
                                    [[2 * S, 64], [1, 128]]),
                            ident[64 * hh:64 * (hh + 1), 64 * hh:64 * (hh + 1)]), inc="T")
                        pb.wait("vector", "T", ct)
                        cv = pb.op("vector", lambda v, h_loc=h_loc, k0=k0: v.tensor_copy(
                            vaug[h_loc][:, k0 * 65:k0 * 65 + 64], ptr[:, 0:64]), inc="V")
                        pb.wait("tensor", "V", cv)
                pb.wait("tensor", "X")
                pb.wait("tensor", "V")
                if dbg and grp == 0:
                    pb.wait("sync", "X")
                    pb.wait("sync", "V")
                    pb.dma("sync", dbg_ka[:], kaug[0][:], "ST")
                    pb.dma("sync", dbg_qa[:], qaug[0][:], "ST")
                    pb.wait("sync", "ST")
                for h_loc in range(4):
                    hglob = grp * 4 + h_loc
                    for qc in range(2):
                        for kq in range(4):
                            nk6 = 4
                            for k6 in range(nk6):
                                k0 = kq * 4 + k6
                                cmm = pb.op("tensor", lambda t, h_loc=h_loc, k0=k0, k6=k6, qc=qc: t.matmul(
                                    sp[:, k6 * 512:(k6 + 1) * 512],
                                    kaug[h_loc][:, k0 * 128:(k0 + 1) * 128],
                                    qaug[h_loc][:, qc * 512:(qc + 1) * 512],
                                    start=True, stop=True), inc="T")
                                for q0g in range(4):
                                    q0 = qc * 4 + q0g
                                    band = None
                                    if k0 == q0:
                                        band = 0
                                    elif k0 == (q0 - 1) % 16:
                                        band = 1 if q0 != 0 else 2
                                    if band is not None:
                                        cmm = pb.op("tensor", lambda t, band=band, h_loc=h_loc, k6=k6, q0g=q0g: t.matmul(
                                            sp[:, k6 * 512 + q0g * 128: k6 * 512 + (q0g + 1) * 128],
                                            ident[:], d0t[h_loc][:, band * 128:(band + 1) * 128],
                                            start=False, stop=True), inc="T")
                            pb.wait("scalar", "T", cmm)
                            cex = pb.op("scalar", lambda s, n=nk6: s.activation(
                                out=atile[:, 0:n * 512],
                                in_=sp[:, 0:n * 512], func=AF.Exp), inc="S")
                            pb.wait("tensor", "S", cex)
                            for k6 in range(nk6):
                                k0 = kq * 4 + k6
                                cmv = pb.op("tensor", lambda t, h_loc=h_loc, k0=k0, k6=k6: t.matmul(
                                    pnum[:], vaug[h_loc][:, k0 * 65:(k0 + 1) * 65],
                                    atile[:, k6 * 512:(k6 + 1) * 512],
                                    start=(k0 == 0), stop=(k0 == 15)), inc="T" if k0 == 15 else None)
                            pb.wait("scalar", "T")
                        pb.wait("vector", "T")
                        cr1 = pb.opw("vector", lambda v: v.reciprocal(zr[:], pnum[64:65, :]), inc="V")
                        cr2 = pb.opw("vector", lambda v: v.tensor_copy(zrb[:], zr[:]), inc="V")
                        pb.wait("tensor", "V", cr2)
                        cbz = pb.op("tensor", lambda t: t.matmul(pzb[:], ones64[:], zrb[:],
                                                                 start=True, stop=True), inc="T")
                        pb.wait("vector", "T", cbz)
                        cz1 = pb.opw("vector", lambda v: v.tensor_copy(zbb[:], pzb[:]), inc="V")
                        cz2 = pb.op("scalar", lambda s: s.copy(nsb[:], pnum[0:64, :]), inc="S")
                        pb.wait("vector", "S", cz2)
                        sub, hh = divmod(h_loc, 2)
                        dta = hglob // 2
                        cfin = pb.opw("vector", lambda v: v.tensor_tensor(
                            out=osc[:], in0=nsb[:], in1=zbb[:], op=ALU.mult), inc="V")
                        pb.wait("tensor", "V", cfin)
                        pb.wait("scalar", "V", cfin)
                        pb.wait("sync", "V", cfin)
                        pb.dma("sync", APClass(attn_dram.tensor,
                               dta * 128 * TO + 64 * hh * TO + qc * 512,
                               [[TO, 64], [1, 512]]), osc[:], "ST")
                        pb.wait("vector", "ST")
            pb.phase_barrier()

        with contextlib.ExitStack() as phW:
            hW = phW.enter_context
            wbw = hW(nc.sbuf_tensor("wbw", [128, 2 * 2048], BF16))
            afb = hW(nc.sbuf_tensor("afb", [128, 2 * TO], BF16))
            pwo2 = hW(nc.psum_tensor("pwo2", [128, 1024], F32))
            for mt in range(16):
                lw = wcol3(wo, mt, wbw[:, (mt % 2) * 2048:(mt % 2 + 1) * 2048])
                pb.wait("tensor", "LD", lw)
                for kt in range(16):
                    pb.wait("sync", "T")
                    la = pb.dma("sync", afb[:, (kt % 2) * TO:(kt % 2 + 1) * TO],
                                attn_dram[kt], "LD2")
                    pb.wait("tensor", "LD2", la)
                    for ch in range(2):
                        cmm = pb.op("tensor", lambda t, kt=kt, ch=ch, mt=mt: t.matmul(
                            pwo2[:, ch * 512:(ch + 1) * 512],
                            wbw[:, (mt % 2) * 2048 + kt * 128:(mt % 2) * 2048 + (kt + 1) * 128],
                            afb[:, (kt % 2) * TO + ch * 512:(kt % 2) * TO + (ch + 1) * 512],
                            start=(kt == 0), stop=(kt == 15)),
                            inc="T" if (ch == 1) else None)
                for ch in range(2):
                    pb.wait("vector", "T")
                    cv = pb.op("vector", lambda v, mt=mt, ch=ch: v.tensor_copy(
                        acc2[:, mt * TO + ch * 512: mt * TO + (ch + 1) * 512],
                        pwo2[:, ch * 512:(ch + 1) * 512]), inc="V")
                    pb.wait("tensor", "V", cv)
                if dbg and mt == 0:
                    pb.wait("sync", "V")
                    pb.dma("sync", dbg_wo2[:, 0:TO], acc2[:, 0:TO], "ST")
                    pb.dma("sync", dbg_wo2[:, TO:TO + TO], afb[:, 0:TO], "ST")
                    pb.wait("sync", "ST")
                if dbg:
                    pb.wait("sync", "V")
                    pb.dma("sync", dbg_pr[:, 2 * mt:2 * mt + 2], acc2[:, 0:2], "ST")
                    pb.wait("sync", "ST")
                pb.wait("gpsimd", "T")
            pb.phase_barrier()


        with contextlib.ExitStack() as phP:
            hP = phP.enter_context
            zw = hP(nc.sbuf_tensor("zw", [128, D], BF16))
            w2b = hP(nc.sbuf_tensor("w2b", [128, D], BF16))
            ps1 = hP(nc.psum_tensor("ps1", [128, 512], F32))
            for at in range(64):
                lw = pb.dma("sync", wfl[:], w2t[at * 128:(at + 1) * 128, :], "LD")
                pb.wait("vector", "LD", lw)
                cw2 = pb.opw("vector", lambda v: v.tensor_copy(w2b[:], wfl[:]), inc="V")
                pb.wait("tensor", "V", cw2)
                for ch in range(4):
                    cmm = pb.op("tensor", lambda t, ch=ch: t.matmul(
                        ps1[:], h128[:], w2b[:, ch * 512:(ch + 1) * 512],
                        start=True, stop=True), inc="T")
                    pb.wait("vector", "T", cmm)
                    cv = pb.op("vector", lambda v, ch=ch: v.tensor_copy(
                        zw[:, ch * 512:(ch + 1) * 512], ps1[:]), inc="V")
                    pb.wait("tensor", "V", cv)
                pb.wait("sync", "V")
                pb.dma("sync", APClass(w2s_dram.tensor, at * D, [[128 * D, 64], [1, D]]),
                       zw[0:64, :], "ST")
                pb.dma("sync", APClass(w2s_dram.tensor, (at + 64) * D, [[128 * D, 64], [1, D]]),
                       zw[64:128, :], "ST")
                pb.wait("vector", "ST")
            for blk in range(64):
                lw = pb.dma("sync", w2b[:], w2s_dram[blk], "LD")
                pb.wait("tensor", "LD", lw)
                for ch in range(4):
                    cmm = pb.op("tensor", lambda t, ch=ch: t.matmul(
                        ps1[:], h64b[:], w2b[:, ch * 512:(ch + 1) * 512],
                        start=True, stop=True), inc="T")
                    pb.wait("vector", "T", cmm)
                    cv = pb.op("vector", lambda v, ch=ch: v.tensor_copy(
                        zw[:, ch * 512:(ch + 1) * 512], ps1[:]), inc="V")
                    pb.wait("tensor", "V", cv)
                pb.wait("sync", "V")
                for dmt in range(16):
                    pb.dma("sync", w2p_dram[dmt][blk], zw[:, dmt * 128:(dmt + 1) * 128], "ST")
                pb.wait("vector", "ST")
                pb.wait("tensor", "ST")
            pb.phase_barrier()

        with contextlib.ExitStack() as phF:
            hF = phF.enter_context
            hsg = hF(nc.sbuf_tensor("hsg", [128, 16 * TO], BF16))
            w13 = hF(nc.sbuf_tensor("w13", [128, 2 * 2048], BF16))
            w2sg = hF(nc.sbuf_tensor("w2sg", [128, 16 * 128], BF16))
            sil = hF(nc.sbuf_tensor("sil", [128, TO], BF16))
            oevb = hF(nc.sbuf_tensor("oevb", [128, 512], BF16))
            pg1 = hF(nc.psum_tensor("pg1", [128, 512], F32))
            pg3 = hF(nc.psum_tensor("pg3", [128, 512], F32))
            po0 = hF(nc.psum_tensor("po0", [128, 512], F32))
            if dbg:
                lwd = pb.dma("sync", w2sg[:, 0:128], w2p_dram[0][0], "LD")
                pb.wait("sync", "LD", lwd)
                pb.dma("sync", dbg_w2[:], w2sg[:, 0:128], "ST")
                pb.wait("sync", "ST")
            for sg in range(4):
                for bj in range(16):
                    blk = sg * 16 + bj
                    lw1 = wcol3(w1pt, blk, w13[:, 0:2048])
                    lw3 = wcol3(w3pt, blk, w13[:, 2048:4096])
                    pb.wait("tensor", "LD", lw3)
                    for ch in range(2):
                        for kt in range(16):
                            pb.op("tensor", lambda t, kt=kt, ch=ch: t.matmul(
                                pg1[:], w13[:, kt * 128:(kt + 1) * 128],
                                xn[:, kt * S + ch * 512: kt * S + (ch + 1) * 512],
                                start=(kt == 0), stop=(kt == 15)))
                        for kt in range(16):
                            cmm = pb.op("tensor", lambda t, kt=kt, ch=ch: t.matmul(
                                pg3[:], w13[:, 2048 + kt * 128: 2048 + (kt + 1) * 128],
                                xn[:, kt * S + ch * 512: kt * S + (ch + 1) * 512],
                                start=(kt == 0), stop=(kt == 15)), inc="T" if kt == 15 else None)
                        pb.wait("scalar", "T", cmm)
                        csl = pb.op("scalar", lambda s, ch=ch: s.activation(
                            out=sil[:, ch * 512:(ch + 1) * 512], in_=pg1[:], func=AF.Silu), inc="S")
                        pb.wait("vector", "S", csl)
                        chb = pb.opw("vector", lambda v, bj=bj, ch=ch: v.tensor_tensor(
                            out=hsg[:, bj * TO + ch * 512: bj * TO + (ch + 1) * 512],
                            in0=sil[:, ch * 512:(ch + 1) * 512], in1=pg3[:], op=ALU.mult), inc="V")
                        pb.wait("tensor", "V", chb)
                    pb.wait("gpsimd", "T")
                if dbg and sg == 0:
                    pb.wait("sync", "V")
                    pb.dma("sync", dbg_hb[:], hsg[:, 0:TO], "ST")
                    pb.wait("sync", "ST")
                for dmt in range(16):
                    lw2 = pb.dma("gpsimd", w2sg[:],
                                 APClass(w2p_dram.tensor, (dmt * 64 + sg * 16) * 16384,
                                         [[128, 128], [128 * 128, 16], [1, 128]]), "LD2")
                    pb.wait("tensor", "LD2", lw2)
                    for ch in range(2):
                        for bj in range(16):
                            cmm = pb.op("tensor", lambda t, bj=bj, ch=ch: t.matmul(
                                po0[:], w2sg[:, bj * 128:(bj + 1) * 128],
                                hsg[:, bj * TO + ch * 512: bj * TO + (ch + 1) * 512],
                                start=(bj == 0), stop=(bj == 15)), inc="T" if bj == 15 else None)
                        pb.wait("scalar", "T", cmm)
                        cev = pb.op("scalar", lambda s: s.copy(oevb[:], po0[:]), inc="S")
                        pb.wait("vector", "S", cev)
                        cv = pb.opw("vector", lambda v, dmt=dmt, ch=ch: v.tensor_tensor(
                            out=acc2[:, dmt * TO + ch * 512: dmt * TO + (ch + 1) * 512],
                            in0=acc2[:, dmt * TO + ch * 512: dmt * TO + (ch + 1) * 512],
                            in1=oevb[:], op=ALU.add), inc="V")
                        pb.wait("tensor", "S", cev)
                    pb.wait("gpsimd", "T")
            if dbg:
                pb.wait("sync", "V")
                pb.dma("sync", dbg_ff[:], acc2[:, 0:TO], "ST")
                pb.wait("sync", "ST")
            pb.phase_barrier()

        with contextlib.ExitStack() as phZ:
            hZ = phZ.enter_context
            x0t = hZ(nc.sbuf_tensor("x0t", [128, D], F32))
            x1t = hZ(nc.sbuf_tensor("x1t", [128, D], F32))
            xf32 = hZ(nc.sbuf_tensor("xf32", [128, 128], F32))
            ptf = hZ(nc.psum_tensor("ptf", [128, 512], BF16))
            for tt in range(8):
                ldx = pb.dma("sync", x0t[:], x0r[tt * 128:(tt + 1) * 128, :], "LD")
                pb.wait("vector", "LD", ldx)
                for dt in range(16):
                    j = dt % 4
                    ct = pb.op("tensor", lambda t, dt=dt, j=j, tt=tt: t.transpose(
                        ptf[:, j * 128:(j + 1) * 128],
                        acc2[:, dt * TO + tt * 128: dt * TO + (tt + 1) * 128], ident[:]), inc="T")
                    pb.wait("scalar", "T", ct)
                    cev = pb.op("scalar", lambda s, j=j: s.copy(xf32[:], ptf[:, j * 128:(j + 1) * 128]), inc="S")
                    pb.wait("vector", "S", cev)
                    pb.wait("tensor", "S", cev)
                    cv = pb.op("vector", lambda v, dt=dt: v.tensor_tensor(
                        out=x1t[:, dt * 128:(dt + 1) * 128],
                        in0=xf32[:],
                        in1=x0t[:, dt * 128:(dt + 1) * 128], op=ALU.add), inc="V")
                    pb.wait("scalar", "V", cv)
                pb.wait("sync", "V")
                pb.dma("sync", y[tt * 128:(tt + 1) * 128, :], x1t[:], "ST")
                pb.wait("vector", "ST")
        pb.wait("sync", "ST")
        pb.wait("gpsimd", "ST")
    return nc




def build_final(nin=11):
    nc = bass.Bass("TRN2", target_bir_lowering=False)
    TOq, D = 1024, 2048
    xs = [nc.dram_tensor(f"in{i}", [TOq, D], F32, kind="ExternalInput").ap()
          for i in range(nin)]
    y = nc.dram_tensor("y", [TOq, D], F32, kind="ExternalOutput").ap()
    nt = TOq // 128
    with program(nc) as (pb, hold):
        acc = hold(nc.sbuf_tensor("acc", [128, nt * D], F32))
        tmp = hold(nc.sbuf_tensor("tmp", [128, nt * D], F32))

        def tile(t, i):
            return t[:, i * D:(i + 1) * D]
        for i in range(nt):
            pb.dma("sync", tile(acc, i), xs[0][i * 128:(i + 1) * 128, :], "LD")
        for j in range(1, nin):
            for i in range(nt):
                pb.dma("sync", tile(tmp, i), xs[j][i * 128:(i + 1) * 128, :], "LD")
            pb.wait("vector", "LD")
            for i in range(nt):
                cv = pb.op("vector", lambda v, i=i: v.tensor_tensor(
                    out=tile(acc, i), in0=tile(acc, i), in1=tile(tmp, i),
                    op=ALU.add), inc="V")
            if j < nin - 1:
                pb.wait("sync", "V")
        pb.wait("sync", "V")
        for i in range(nt):
            pb.dma("sync", y[i * 128:(i + 1) * 128, :], tile(acc, i), "ST")
        pb.wait("sync", "ST")
    return nc

def _bucket(d):
    rp = np.maximum(d, 0)
    is_small = rp < 16
    with np.errstate(divide="ignore"):
        rl = 16 + (np.log(np.maximum(rp, 1).astype(np.float64) / 16) / math.log(8) * 16).astype(np.int64)
    return np.where(is_small, rp, np.minimum(rl, 31))


def _l0_common(Wq, Wk, Wv, Wo, W1, W2, W3, rel_bias):
    jj, qq = np.meshgrid(np.arange(64), np.arange(128), indexing="ij")
    hidx = (128 * (qq % 64) + 64 * (qq // 64) + jj).reshape(-1)
    H128 = (-1.0) ** np.array([bin(a & b).count("1") for a in range(128) for b in range(128)],
                              dtype=np.float64).reshape(128, 128)
    H64 = H128[:64, :64]
    h64in = np.zeros((128, 128), np.float32)
    h64in[:64, :64] = H64
    h64in[64:, 64:] = H64
    jj2, ii2 = np.meshgrid(np.arange(128), np.arange(128), indexing="ij")
    ind0 = np.zeros((32, 128 * 128), np.float32)
    ind1 = np.zeros((32, 128 * 128), np.float32)
    b0m = _bucket(ii2 - jj2)
    b1m = _bucket(ii2 - jj2 + 128)
    for c in range(32):
        ind0[c] = (b0m == c).reshape(-1)
        ind1[c] = (b1m == c).reshape(-1)
    qpat = np.zeros((16, 1024), np.float32)
    for q0 in range(8):
        qpat[q0, q0 * 128:(q0 + 1) * 128] = 1.0
    kpat = np.zeros((16, 2048), np.float32)
    for j in range(16):
        kpat[j, j * 128:(j + 1) * 128] = 1.0
    onepair = np.zeros((128, 2), np.float32)
    onepair[0:64, 0] = 1.0
    onepair[64:128, 1] = 1.0
    return {
        "wq": Wq, "wk": Wk, "wv": Wv, "wo": Wo,
        "w1pt": np.ascontiguousarray(W1[hidx].T),
        "w3pt": np.ascontiguousarray(W3[hidx].T),
        "w2t": np.ascontiguousarray(W2.T),
        "relb": rel_bias,
        "relb01": np.ascontiguousarray(rel_bias[[0, 31], :]),
        "ind0": ind0, "ind1": ind1,
        "qpat": qpat.astype(BF), "kpat16": kpat.astype(BF),
        "onepair": onepair, "identin3": np.eye(128, dtype=np.float32),
        "h128in": (H128 * 2 ** -6.5).astype(np.float32),
        "h64in": h64in,
    }


def _l0_percore(common, x0b, gamma):
    own = slice(gamma * 1024, gamma * 1024 + 1024)
    oth = slice((1 - gamma) * 1024, (1 - gamma) * 1024 + 1024)
    amap = np.zeros((2, 128), np.float32)
    for q0 in range(8):
        for k0 in range(16):
            i = q0 * 16 + k0
            if k0 == q0:
                continue
            if k0 == (q0 - 1) % 16:
                if q0 == 0 and gamma == 0:
                    amap[0, i] = 1.0
                continue
            if k0 < 8:
                amap[0 if k0 > q0 else 1, i] = 1.0
            else:
                amap[0 if gamma == 0 else 1, i] = 1.0
    d = dict(common)
    d["x0r"] = np.ascontiguousarray(np.concatenate([x0b[own], x0b[oth]], 0))
    d["amap"] = amap
    d["indw"] = common["ind1"] if gamma == 1 else np.zeros_like(common["ind1"])
    return d


def _mamba_percore(x1b, W_in, conv_w, conv_b, W_x, W_dt, b_dt, A_log, Dv, W_out, dh):
    half = np.arange(1024 * dh, 1024 * dh + 1024)
    other = np.arange(1024 * (1 - dh), 1024 * (1 - dh) + 1024)
    roll = np.concatenate([half, other])
    return {
        "xb": x1b,
        "w_in_x": np.ascontiguousarray(W_in[:, :2048]),
        "w_in_r": np.ascontiguousarray(W_in[:, 2048 + half]),
        **{f"cw{k}": np.ascontiguousarray(conv_w[roll, :, k].T) for k in range(4)},
        "cbT": np.ascontiguousarray(conv_b[roll][:, None]),
        "w_x": np.ascontiguousarray(W_x[roll, :]),
        "w_dt_h": np.ascontiguousarray(W_dt[:, half]),
        "b_dt_h": np.ascontiguousarray(b_dt[half].reshape(8, 128).T),
        "a_log_h": np.ascontiguousarray(A_log[half]),
        "d_h": np.ascontiguousarray(Dv[half].reshape(8, 128).T),
        "w_out_h": np.ascontiguousarray(W_out[half][:, roll]),
        "identin2": np.eye(128, dtype=np.float32),
    }, roll


def _ffn1_percore(x1full, gpk, gsc, upk, usc, dpk, dsc, c):
    idx = np.concatenate([np.arange(2048 * c, 2048 * (c + 1), 2),
                          np.arange(2048 * c + 1, 2048 * (c + 1), 2)])
    return {
        "x1": x1full,
        "gpk": np.ascontiguousarray(gpk[idx].T),
        "upk": np.ascontiguousarray(upk[idx].T),
        "dpk": np.ascontiguousarray(dpk[:, 1024 * c:1024 * (c + 1)].T),
        "gsc": np.ascontiguousarray(gsc[idx].T),
        "usc": np.ascontiguousarray(usc[idx].T),
        "dsc": np.ascontiguousarray(dsc[:, 64 * c:64 * (c + 1)].T),
        "identin": np.eye(128, dtype=np.float32),
    }


def kernel(**inputs):
    f32 = np.float32
    x0 = np.asarray(inputs["L0_attn_hidden_states"], f32)
    common = _l0_common(np.asarray(inputs["L0_attn_Wq"], f32),
                        np.asarray(inputs["L0_attn_Wk"], f32),
                        np.asarray(inputs["L0_attn_Wv"], f32),
                        np.asarray(inputs["L0_attn_Wo"], f32),
                        np.asarray(inputs["L0_ffn_W1"], f32),
                        np.asarray(inputs["L0_ffn_W2"], f32),
                        np.asarray(inputs["L0_ffn_W3"], f32),
                        np.asarray(inputs["L0_attn_rel_bias"], f32))
    nc0 = build_l0()
    maps = [_l0_percore(common, x0[c // 2], c % 2) for c in range(8)]
    res0 = run_bass_kernel_spmd(nc0, maps, core_ids=list(range(8)))
    x1 = np.empty((4, 2048, 2048), f32)
    for c in range(8):
        x1[c // 2, (c % 2) * 1024:(c % 2) * 1024 + 1024] = res0.results[c]["y"]

    nc2 = build_mamba()
    mmaps, rolls = [], []
    for c in range(8):
        m, roll = _mamba_percore(x1[c // 2],
                                 np.asarray(inputs["L1_attn_W_in"], f32),
                                 np.asarray(inputs["L1_attn_conv_w"], f32),
                                 np.asarray(inputs["L1_attn_conv_b"], f32),
                                 np.asarray(inputs["L1_attn_W_x"], f32),
                                 np.asarray(inputs["L1_attn_W_dt"], f32),
                                 np.asarray(inputs["L1_attn_b_dt"], f32),
                                 np.asarray(inputs["L1_attn_A_log"], f32),
                                 np.asarray(inputs["L1_attn_D"], f32),
                                 np.asarray(inputs["L1_attn_W_out"], f32),
                                 c % 2)
        mmaps.append(m)
        rolls.append(roll)
    res2 = run_bass_kernel_spmd(nc2, mmaps, core_ids=list(range(8)))
    mo = []
    for c in range(8):
        m = res2.results[c]["mo"]
        un = np.empty_like(m)
        un[rolls[c]] = m
        mo.append(np.ascontiguousarray(un.T))

    nc3 = build_ffn1()
    x1full = np.ascontiguousarray(x1.reshape(8192, 2048))
    fmaps = [_ffn1_percore(x1full,
                           np.asarray(inputs["L1_ffn_gate_W_packed"], np.int32),
                           np.asarray(inputs["L1_ffn_gate_scales"], f32),
                           np.asarray(inputs["L1_ffn_up_W_packed"], np.int32),
                           np.asarray(inputs["L1_ffn_up_scales"], f32),
                           np.asarray(inputs["L1_ffn_down_W_packed"], np.int32),
                           np.asarray(inputs["L1_ffn_down_scales"], f32), c)
             for c in range(8)]
    res3 = run_bass_kernel_spmd(nc3, fmaps, core_ids=list(range(8)))
    po = [np.ascontiguousarray(res3.results[c]["po"].T) for c in range(8)]

    nc4 = build_final(nin=11)
    gmaps = []
    for c in range(8):
        b, g = c // 2, c % 2
        rs = slice(b * 2048 + g * 1024, b * 2048 + g * 1024 + 1024)
        ls = slice(g * 1024, g * 1024 + 1024)
        ins = [np.ascontiguousarray(x1[b][ls]),
               np.ascontiguousarray(mo[2 * b][b * 0 + 0:0] if False else mo[2 * b][ls]),
               np.ascontiguousarray(mo[2 * b + 1][ls])]
        ins += [np.ascontiguousarray(po[j][rs]) for j in range(8)]
        gmaps.append({f"in{i}": ins[i] for i in range(11)})
    res4 = run_bass_kernel_spmd(nc4, gmaps, core_ids=list(range(8)))
    out = np.empty((4, 2048, 2048), f32)
    for c in range(8):
        out[c // 2, (c % 2) * 1024:(c % 2) * 1024 + 1024] = res4.results[c]["y"]
    return out

